# version 04
# baseline (speedup 1.0000x reference)
import numpy as np
import concourse.bass as bass
import concourse.mybir as mybir
from concourse.alu_op_type import AluOpType
from concourse.bass_utils import run_bass_kernel_spmd

F32 = mybir.dt.float32
BF16 = mybir.dt.bfloat16
AF = mybir.ActivationFunctionType
AX = mybir.AxisListType.X
OP = AluOpType

S, D, H, DK, DL, DR, DFF = 2048, 2048, 16, 128, 512, 64, 8192
TC, TB, NG = 16, 4, 4
M32 = float(np.float32(1.5 * 2 ** 23))
M64 = float(1.5 * 2 ** 52)
PI_LO = float(np.nextafter(np.float32(np.pi), np.float32(0.0)))
C_SCALE = float(np.float32(1.0 / np.float32(np.sqrt(128.0))))
EXP_SHIFT = -10.0

LN1T = 0
KTOK = 0
KGT = 8192
VG = 16384
PTB = 24704
WKG = 25728
WVG = 27776
MSK = 29824
LNTOKS = 32768
QAT = 32768
QTOK = 65536
WQS = 66560
WKV2 = 65536
LATT = 69632
AVT = 77824
AVF = 78336
LN2T = 0
AT = 8192
ATTS = 40960
WOS = 49152
W1S = 65536
W2S = 69632
LN2TOK = 73728
XS = 0
SQF = 4096
C4 = 6144
S4 = 8192
TSC = 10240
X2B = 0
XD = 8192
XQ = 10240
ZS = 11264
ONES, TBL, TCB, QWC, KWC = 0, 1, 2, 3, 4
SXC, SXX, MUC, VTMP, SDC, RC = 8, 9, 10, 11, 12, 13
TPC = 14
ESH = 15
LQ, RL = 16, 24
RQ, RK, SSQ = 32, 96, 160
QWT = 256


def build_nc():
    nc = bass.Bass("TRN2", target_bir_lowering=False)
    x_h = nc.dram_tensor("x", [S, D], F32, kind="ExternalInput").ap()
    wq_h = nc.dram_tensor("wq", [D, H * DK], F32, kind="ExternalInput").ap()
    wkv_h = nc.dram_tensor("wkv", [D, DL], F32, kind="ExternalInput").ap()
    wk_h = nc.dram_tensor("wk", [DL, H * DK], F32, kind="ExternalInput").ap()
    wv_h = nc.dram_tensor("wv", [DL, H * DK], F32, kind="ExternalInput").ap()
    wo_h = nc.dram_tensor("wo", [H * DK, D], F32, kind="ExternalInput").ap()
    w1p_h = nc.dram_tensor("w1p", [64, 128, 16 * 128], F32, kind="ExternalInput").ap()
    w2t_h = nc.dram_tensor("w2t", [DFF, D], F32, kind="ExternalInput").ap()
    qw_h = nc.dram_tensor("qw", [DK, 1], F32, kind="ExternalInput").ap()
    qwr_h = nc.dram_tensor("qwr", [1, DK], F32, kind="ExternalInput").ap()
    kw_h = nc.dram_tensor("kw", [DK, 1], F32, kind="ExternalInput").ap()
    out_h = nc.dram_tensor("out", [S, D], F32, kind="ExternalOutput").ap()
    wob_h = nc.dram_tensor("wob", [H * DK, D], BF16, kind="Internal").ap()
    w1b_h = nc.dram_tensor("w1b", [64, 128, 16 * 128], BF16, kind="Internal").ap()
    w2b_h = nc.dram_tensor("w2b", [DFF, D], BF16, kind="Internal").ap()
    attn_h = nc.dram_tensor("attnh", [H, DK, S], BF16, kind="Internal").ap()

    plan = {"sync": [], "vector": [], "scalar": [], "gpsimd": [], "tensor": []}

    def em(q, fn):
        plan[q].append(fn)

    cnt = {}

    def inc(name, n=1):
        cnt[name] = cnt.get(name, 0) + n
        return cnt[name]

    def cur(name):
        return cnt.get(name, 0)

    free_at = {}

    import contextlib
    with contextlib.ExitStack() as _stk:
        abf = _stk.enter_context(nc.sbuf_tensor("abf", [128, 79360], BF16))
        af32 = _stk.enter_context(nc.sbuf_tensor("af32", [128, 11776], F32))
        sm = _stk.enter_context(nc.sbuf_tensor("sm", [128, 384], F32))
        ident = _stk.enter_context(nc.sbuf_tensor("ident", [128, 128], BF16))
        rw = _stk.enter_context(nc.sbuf_tensor("rw", [1, 640], F32))
        p0 = _stk.enter_context(nc.psum_tensor("p0", [128, 512], F32))
        p1 = _stk.enter_context(nc.psum_tensor("p1", [128, 512], F32))
        p2 = _stk.enter_context(nc.psum_tensor("p2", [128, 512], F32))
        p3 = _stk.enter_context(nc.psum_tensor("p3", [128, 512], F32))
        pavA = _stk.enter_context(nc.psum_tensor("pavA", [128, 512], F32))
        pavB = _stk.enter_context(nc.psum_tensor("pavB", [128, 512], F32))
        pt0 = _stk.enter_context(nc.psum_tensor("pt0", [128, 1024], BF16))
        pt1 = _stk.enter_context(nc.psum_tensor("pt1", [128, 1024], BF16))
        dS = _stk.enter_context(nc.semaphore("dS"))
        dG = _stk.enter_context(nc.semaphore("dG"))
        dA = _stk.enter_context(nc.semaphore("dA"))
        dO = _stk.enter_context(nc.semaphore("dO"))
        dT = _stk.enter_context(nc.semaphore("dT"))
        dQ = _stk.enter_context(nc.semaphore("dQ"))
        dX = [_stk.enter_context(nc.semaphore(f"dX{i}")) for i in range(2)]
        dW = [_stk.enter_context(nc.semaphore(f"dW{i}")) for i in range(2)]
        dV = [_stk.enter_context(nc.semaphore(f"dV{i}")) for i in range(2)]
        dg = [_stk.enter_context(nc.semaphore(f"dg{i}")) for i in range(4)]
        dTs = [_stk.enter_context(nc.semaphore(f"dTs{i}")) for i in range(2)]
        dY = _stk.enter_context(nc.semaphore("dY"))
        kx = _stk.enter_context(nc.semaphore("kx"))
        vv = _stk.enter_context(nc.semaphore("vv"))
        ss = _stk.enter_context(nc.semaphore("ss"))
        pp = _stk.enter_context(nc.semaphore("pp"))
        gg = _stk.enter_context(nc.semaphore("gg"))
        kq = _stk.enter_context(nc.semaphore("kq"))
        ke = _stk.enter_context(nc.semaphore("ke"))
        km = _stk.enter_context(nc.semaphore("km"))
        ka = _stk.enter_context(nc.semaphore("ka"))
        PQ = [p0, p1, p2, p3]
        PT = [pt0, pt1]
        SEM = {"vv": vv, "ss": ss, "pp": pp, "gg": gg, "dS": dS, "dG": dG,
               "dA": dA, "dO": dO, "dT": dT, "dQ": dQ, "kq": kq, "ke": ke, "km": km, "ka": ka}

        def A(off, w):
            return abf[:, off:off + w]

        def Fa(off, w):
            return af32[:, off:off + w]

        def SMc(off, w=1):
            return sm[:, off:off + w]

        def wait_free(eng, res):
            if res in free_at:
                sname, c = free_at[res]
                eng.wait_ge(SEM[sname], c)

        def set_free(res, sname, c):
            free_at[res] = (sname, c)

        def ap3(apx, off_extra, dims):
            return bass.AP(apx.tensor, apx.offset + off_extra, dims)

        def e_sy0(sy):
            sy.dma_start(out=sm[:, QWC:QWC + 1], in_=qw_h).then_inc(dQ, 16)
            sy.dma_start(out=sm[:, KWC:KWC + 1], in_=kw_h).then_inc(dQ, 16)
            sy.dma_start(out=rw[:, 448:576], in_=qwr_h).then_inc(dQ, 16)
        em("sync", e_sy0)
        c_small = inc("dQ", 48)

        g_snap = cur("gg")

        def e_gp0(g, g_snap=g_snap):
            g.memset(ident[:], 1.0)
            g.iota(af32[0:1, C4:C4 + 2048], pattern=[[1, 2048]], base=0,
                   channel_multiplier=0, allow_small_or_imprecise_dtypes=True)
            g.iota(rw[:, 0:32], pattern=[[1, 32]], base=0, channel_multiplier=0,
                   allow_small_or_imprecise_dtypes=True).then_inc(gg, 1)
            g.wait_ge(gg, g_snap + 1)
            g.affine_select(ident[:], ident[:], pattern=[[-1, 128]],
                            compare_op=OP.is_equal, fill=0.0,
                            base=0, channel_multiplier=1).then_inc(gg, 1)
        em("gpsimd", e_gp0)
        c_ident = inc("gg", 2)

        v_snap = cur("vv")

        def e_ve0(v, v_snap=v_snap, wg=c_ident):
            v.wait_ge(gg, wg)
            v.memset(SMc(ONES), 1.0)
            v.memset(SMc(ESH), EXP_SHIFT)
            v.memset(rw[:, 64:65], 10000.0)
            v.memset(rw[:, 65:66], 15.0)
            v.memset(rw[:, 66:67], 1.0)
            v.memset(rw[:, 144:160], 0.0)
            v.memset(rw[:, 320:448], 1.0).then_inc(vv, 1)
            v.wait_ge(vv, v_snap + 1)
            v.tensor_scalar(SMc(TBL), SMc(ONES), M64, -M64, OP.add, OP.add)
            v.reciprocal(rw[:, 67:68], rw[:, 65:66])
            v.drain()
            v.tensor_scalar(SMc(TCB), SMc(TBL), -1.0, 1.0, OP.mult, OP.add).then_inc(vv, 1)
        em("vector", e_ve0)
        c_v0 = inc("vv", 2)

        def e_sc0(s, w=c_v0):
            s.wait_ge(vv, w)
            s.activation(rw[:, 68:69], rw[:, 64:65], AF.Ln)
            s.activation(rw[:, 69:70], rw[:, 66:67], AF.Arctan).then_inc(ss, 1)
        em("scalar", e_sc0)
        c_s0 = inc("ss", 1)

        def e_ve1(v, w=c_s0, wg=c_ident):
            v.wait_ge(ss, w)
            v.wait_ge(gg, wg)
            v.tensor_scalar(rw[:, 70:71], rw[:, 69:70], 8.0, None, OP.mult)
            v.tensor_tensor(rw[:, 72:73], rw[:, 67:68], rw[:, 68:69], OP.mult)
            v.drain()
            v.reciprocal(rw[:, 71:72], rw[:, 70:71])
            v.drain()
            v.tensor_scalar(rw[:, 96:128], rw[:, 0:32], rw[:, 72:73], None,
                            OP.mult).then_inc(vv, 1)
        em("vector", e_ve1)
        c_v1 = inc("vv", 1)

        def e_sc1(s, w=c_v1):
            s.wait_ge(vv, w)
            s.activation(rw[:, 128:160], rw[:, 96:128], AF.Exp, scale=-1.0).then_inc(ss, 1)
        em("scalar", e_sc1)
        c_s1 = inc("ss", 1)

        def e_ve2(v, w=c_s1):
            v.wait_ge(ss, w)
            v.tensor_scalar(rw[:, 160:192], rw[:, 128:160], rw[:, 71:72], None,
                            OP.mult)
            v.drain()
            v.memset(rw[:, 176:192], 0.0).then_inc(vv, 1)
        em("vector", e_ve2)
        c_v2 = inc("vv", 1)

        def e_pe0(t, w1=c_v2, w2=c_small):
            t.wait_ge(vv, w1)
            t.wait_ge(dQ, w2)
            t.matmul(p3[:, 0:1], rw[0:1, 320:448], rw[0:1, 70:71], start=True, stop=True)
            t.matmul(p3[:, 128:256], rw[0:1, 320:448], rw[0:1, 448:576],
                     start=True, stop=True).then_inc(pp, 1)
        em("tensor", e_pe0)
        c_t0 = inc("pp", 1)

        def e_ve3(v, w=c_t0):
            v.wait_ge(pp, w)
            v.tensor_copy(SMc(TPC), p3[:, 0:1])
            v.tensor_copy(sm[:, QWT:QWT + 128], p3[:, 128:256]).then_inc(vv, 1)
        em("vector", e_ve3)
        c_qwt = inc("vv", 1)

        for tc in range(TC):
            def e_pet(t, tc=tc, prev=cnt.get(f"tblv{tc-2}", 0), c_qwt=c_qwt):
                if tc >= 2:
                    t.wait_ge(vv, prev)
                if tc == 1:
                    t.wait_ge(vv, c_qwt)
                t.matmul((p2 if tc % 2 == 0 else p3)[:, 0:32],
                         af32[0:1, C4 + tc * 128:C4 + (tc + 1) * 128], rw[0:1, 160:192],
                         start=True, stop=True).then_inc(pp, 1)
            em("tensor", e_pet)
            c_mt = inc("pp", 1)

            sin_cnts = []
            for si in range(2):
                inc("vv", 1)
                c_th = cur("vv")

                def e_sct(s, c_th=c_th):
                    s.wait_ge(vv, c_th)
                    s.activation(Fa(TSC, 32), Fa(TSC + 384, 32), AF.Sin).then_inc(ss, 1)
                em("scalar", e_sct)
                sin_cnts.append(inc("ss", 1))

            def e_vet(v, tc=tc, c_mt=c_mt, sin_cnts=tuple(sin_cnts)):
                v.wait_ge(pp, c_mt)
                for si, (shift, dst) in enumerate(((0.0, S4), (0.25, C4))):
                    src = (p2 if tc % 2 == 0 else p3)[:, 0:32]
                    v.tensor_scalar(Fa(TSC, 32), src, shift, None, OP.add)
                    v.drain()
                    v.tensor_scalar(Fa(TSC + 128, 32), Fa(TSC, 32), M32, -M32, OP.add, OP.add)
                    v.tensor_scalar(Fa(TSC + 256, 32), Fa(TSC, 32), M64, -M64, OP.add, OP.add)
                    v.drain()
                    v.scalar_tensor_tensor(Fa(TSC + 384, 32), Fa(TSC + 128, 32), SMc(TCB),
                                           Fa(TSC + 256, 32), OP.mult, OP.add)
                    v.drain()
                    v.tensor_tensor(Fa(TSC + 128, 32), Fa(TSC, 32), Fa(TSC + 384, 32),
                                    OP.subtract)
                    v.drain()
                    v.tensor_scalar(Fa(TSC + 256, 32), Fa(TSC + 128, 32), SMc(TPC), None,
                                    OP.mult)
                    v.drain()
                    v.tensor_scalar(Fa(TSC + 384, 32), Fa(TSC + 256, 32), PI_LO, -PI_LO,
                                    OP.min, OP.max).then_inc(vv, 1)
                    v.wait_ge(ss, sin_cnts[si])
                    for hh in range(4):
                        cp = v.tensor_copy(Fa(dst + tc * 128 + hh * 32, 32), Fa(TSC, 32))
                    v.drain()
                cp.then_inc(vv, 1)
            em("vector", e_vet)
            cnt[f"tblv{tc}"] = inc("vv", 1)
        c_tables = cur("vv")

        for tc in range(TC):
            w_ap = cnt.get(f"ln1ap{tc-2}", 0)
            w_sq = cnt.get(f"ln1sq{tc-2}", 0)

            def e_sy1(sy, tc=tc, w_ap=w_ap, w_sq=w_sq, c_small=c_small):
                if tc >= 2:
                    sy.wait_ge(vv, w_ap)
                    sy.wait_ge(ss, w_sq)
                else:
                    sy.wait_ge(dQ, c_small)
                sy.dma_start(out=Fa(XS + (tc % 2) * 2048, 2048),
                             in_=x_h[tc * 128:(tc + 1) * 128, :]).then_inc(dX[tc % 2], 16)
            em("sync", e_sy1)
            xw = inc(f"dX{tc % 2}", 16)
            xsem = tc % 2

            w_red = cnt.get(f"ln1red{tc-1}", 0)

            def e_sc1a(s, tc=tc, xw=xw, w_red=w_red, xsem=xsem):
                s.wait_ge(dX[xsem], xw)
                if tc >= 1:
                    s.wait_ge(vv, w_red)
                s.activation(Fa(SQF, 2048), Fa(XS + (tc % 2) * 2048, 2048),
                             AF.Square).then_inc(ss, 1)
            em("scalar", e_sc1a)
            cnt[f"ln1sq{tc}"] = inc("ss", 1)
            c_sq = cur("ss")

            def e_ve1a(v, tc=tc, xw=xw, c_sq=c_sq, xsem=xsem):
                v.wait_ge(dX[xsem], xw)
                v.wait_ge(ss, c_sq)
                v.reduce_sum(SMc(SXC), Fa(XS + (tc % 2) * 2048, 2048), axis=AX)
                v.reduce_sum(SMc(SXX), Fa(SQF, 2048), axis=AX).then_inc(vv, 1)
                v.drain()
                v.tensor_scalar(SMc(MUC), SMc(SXC), 1.0 / 2048.0, None, OP.mult)
                v.tensor_scalar(SMc(VTMP), SMc(SXX), 1.0 / 2048.0, None, OP.mult)
                v.drain()
                v.scalar_tensor_tensor(SMc(SDC), SMc(MUC), SMc(MUC), SMc(VTMP),
                                       OP.mult, OP.subtract)
                v.drain()
                v.tensor_scalar(SMc(SDC), SMc(SDC), -1.0, 1e-5,
                                OP.mult, OP.add).then_inc(vv, 1)
            em("vector", e_ve1a)
            cnt[f"ln1red{tc}"] = inc("vv", 1)
            c_var = inc("vv", 1)

            def e_sc1b(s, c_var=c_var):
                s.wait_ge(vv, c_var)
                s.activation(SMc(SDC), SMc(SDC), AF.Sqrt).then_inc(ss, 1)
            em("scalar", e_sc1b)
            c_sd = inc("ss", 1)

            def e_ve1b(v, tc=tc, c_sd=c_sd):
                v.wait_ge(ss, c_sd)
                v.reciprocal(SMc(RC), SMc(SDC))
                v.drain()
                v.tensor_scalar(A(LNTOKS, 2048), Fa(XS + (tc % 2) * 2048, 2048),
                                SMc(MUC), SMc(RC), OP.subtract, OP.mult).then_inc(vv, 1)
            em("vector", e_ve1b)
            cnt[f"ln1ap{tc}"] = inc("vv", 1)
            c_ap = cur("vv")

            for b in range(4):
                half = b % 2

                def e_pe1(t, tc=tc, b=b, half=half, c_ap=c_ap,
                          ptw=free_at.get(f"pt{b % 2}")):
                    t.wait_ge(vv, c_ap)
                    if ptw:
                        t.wait_ge(SEM[ptw[0]], ptw[1])
                    for i in range(4):
                        dc = b * 4 + i
                        mt = t.transpose(PT[half][:, i * 128:(i + 1) * 128],
                                         A(LNTOKS + dc * 128, 128), ident[:])
                        if i == 3:
                            mt.then_inc(pp, 1)
                em("tensor", e_pe1)
                c_tp = inc("pp", 1)

                def e_ve1c(v, tc=tc, b=b, half=half, c_tp=c_tp):
                    v.wait_ge(pp, c_tp)
                    for i in range(4):
                        dc = b * 4 + i
                        cp = v.tensor_copy(A(LN1T + dc * 2048 + tc * 128, 128),
                                           PT[half][:, i * 128:(i + 1) * 128])
                    cp.then_inc(vv, 1)
                em("vector", e_ve1c)
                set_free(f"pt{half}", "vv", inc("vv", 1))
        c_ln1 = cur("vv")

        for gi in range(NG):
            w_prev = cnt.get(f"qproj{gi-1}", 0)

            sl = gi % 2
            prior = cur(f"dg{sl}")

            def e_gp3(g, gi=gi, w_prev=w_prev, c_id=c_ident, sl=sl, prior=prior):
                if gi >= 1:
                    g.wait_ge(pp, w_prev)
                else:
                    g.wait_ge(gg, c_id)
                g.wait_ge(dg[sl], prior)
                for dc in range(16):
                    g.dma_start(out=A(WQS + dc * 512, 512),
                                in_=wq_h[dc * 128:(dc + 1) * 128,
                                         gi * 512:(gi + 1) * 512]).then_inc(dg[sl], 16)
            em("gpsimd", e_gp3)
            c_wq = inc(f"dg{sl}", 256)
            wq_sl = sl

            for tc in range(TC):
                w_ev = cnt.get(f"qevac{gi}_{tc-2}", cnt.get(f"qevac{gi-1}_{14+tc}", 0))
                w_sq = cnt.get(f"qsq{gi}_{tc-2}", cnt.get(f"qsq{gi-1}_{14+tc}", 0))

                def e_pe3(t, gi=gi, tc=tc, c_wq=c_wq, w_ev=w_ev, w_sq=w_sq,
                          c_ln1=c_ln1, wq_sl=wq_sl):
                    if tc == 0:
                        t.wait_ge(dg[wq_sl], c_wq)
                        if gi == 0:
                            t.wait_ge(vv, c_ln1)
                    if gi > 0 or tc >= 2:
                        t.wait_ge(vv, w_ev)
                        t.wait_ge(ss, w_sq)
                    ps = PQ[tc % 2]
                    for dc in range(16):
                        mm = t.matmul(ps[:, 0:512],
                                      A(LN1T + dc * 2048 + tc * 128, 128),
                                      A(WQS + dc * 512, 512),
                                      start=(dc == 0), stop=(dc == 15))
                        if dc == 15:
                            mm.then_inc(pp, 1)
                em("tensor", e_pe3)
                c_qp = inc("pp", 1)

                w_red = cnt.get(f"qred{gi}_{tc-1}", cnt.get(f"qred{gi-1}_15", 0))

                def e_sc3(s, gi=gi, tc=tc, c_qp=c_qp, w_red=w_red,
                          w_evac=cur("vv") + 1):
                    if gi > 0 or tc >= 1:
                        s.wait_ge(vv, w_red)
                    s.wait_ge(vv, w_evac)
                    s.activation(Fa(SQF, 512), A(QTOK + (tc % 2) * 512, 512),
                                 AF.Square).then_inc(ss, 1)
                em("scalar", e_sc3)
                cnt[f"qsq{gi}_{tc}"] = inc("ss", 1)
                c_qs = cur("ss")

                def e_ve3a(v, gi=gi, tc=tc, c_qp=c_qp, c_qs=c_qs):
                    v.wait_ge(pp, c_qp)
                    v.tensor_copy(A(QTOK + (tc % 2) * 512, 512),
                                  PQ[tc % 2][:, 0:512]).then_inc(vv, 1)
                    v.wait_ge(ss, c_qs)
                    for hh in range(4):
                        v.tensor_reduce(SMc(SSQ + tc * 4 + hh), Fa(SQF + hh * 128, 128),
                                        axis=AX, op=OP.add)
                    v.drain()
                    v.tensor_scalar(sm[:, SSQ + tc * 4:SSQ + tc * 4 + 4],
                                    sm[:, SSQ + tc * 4:SSQ + tc * 4 + 4],
                                    1.0 / 128.0, 1e-6, OP.mult, OP.add).then_inc(vv, 1)
                em("vector", e_ve3a)
                cnt[f"qevac{gi}_{tc}"] = inc("vv", 1)
                cnt[f"qred{gi}_{tc}"] = inc("vv", 1)
                c_qst = cur("vv")

                def e_sc3rq(s, tc=tc, c_qst=c_qst):
                    s.wait_ge(vv, c_qst)
                    s.activation(sm[:, SSQ + tc * 4:SSQ + tc * 4 + 4],
                                 sm[:, SSQ + tc * 4:SSQ + tc * 4 + 4],
                                 AF.Sqrt).then_inc(ss, 1)
                em("scalar", e_sc3rq)
                c_qsd = inc("ss", 1)

                def e_ve3d(v, gi=gi, tc=tc, c_qsd=c_qsd, c_qwt=c_qwt, c_tab=c_tables):
                    if gi == 0 and tc == 0:
                        v.wait_ge(vv, c_qwt)
                        v.wait_ge(vv, c_tab)
                    v.wait_ge(ss, c_qsd)
                    v.reciprocal(sm[:, RQ + tc * 4:RQ + tc * 4 + 4],
                                 sm[:, SSQ + tc * 4:SSQ + tc * 4 + 4])
                    v.drain()
                    qb0 = QTOK + (tc % 2) * 512
                    for hh in range(4):
                        v.scalar_tensor_tensor(
                            A(qb0 + hh * 128, 128), A(qb0 + hh * 128, 128),
                            SMc(RQ + tc * 4 + hh),
                            sm[:, QWT:QWT + 128], OP.mult, OP.mult)
                    base = A(qb0, 512)
                    pstr = base.ap[0][0]
                    x1 = ap3(base, 0, [[pstr, 128], [128, 4], [1, 32]])
                    x2 = ap3(base, 32, [[pstr, 128], [128, 4], [1, 32]])
                    fb = Fa(C4 + tc * 128, 128)
                    fstr = fb.ap[0][0]
                    ct = ap3(fb, 0, [[fstr, 128], [32, 4], [1, 32]])
                    st = ap3(Fa(S4 + tc * 128, 128), 0, [[fstr, 128], [32, 4], [1, 32]])
                    t1 = ap3(Fa(TSC, 128), 0, [[fstr, 128], [32, 4], [1, 32]])
                    t2 = ap3(Fa(TSC + 128, 128), 0, [[fstr, 128], [32, 4], [1, 32]])
                    t3 = ap3(Fa(TSC + 256, 128), 0, [[fstr, 128], [32, 4], [1, 32]])
                    t4 = ap3(Fa(TSC + 384, 128), 0, [[fstr, 128], [32, 4], [1, 32]])
                    v.drain()
                    v.tensor_tensor(t1, x1, ct, OP.mult)
                    v.tensor_tensor(t2, x2, st, OP.mult)
                    v.tensor_tensor(t3, x2, ct, OP.mult)
                    v.tensor_tensor(t4, x1, st, OP.mult)
                    v.drain()
                    v.tensor_tensor(x1, t1, t2, OP.add)
                    v.tensor_tensor(x2, t3, t4, OP.subtract).then_inc(vv, 1)
                em("vector", e_ve3d)
                c_qr = inc("vv", 1)

                half = tc % 2

                def e_pe3b(t, tc=tc, half=half, c_qr=c_qr,
                           ptw=free_at.get(f"pt{tc % 2}")):
                    t.wait_ge(vv, c_qr)
                    if ptw:
                        t.wait_ge(SEM[ptw[0]], ptw[1])
                    qb0 = QTOK + (tc % 2) * 512
                    for hh in range(4):
                        mt = t.transpose(PT[half][:, hh * 128:(hh + 1) * 128],
                                         A(qb0 + hh * 128, 128), ident[:])
                        if hh == 3:
                            mt.then_inc(pp, 1)
                em("tensor", e_pe3b)
                c_qt = inc("pp", 1)

                def e_ve3e(v, gi=gi, tc=tc, half=half, c_qt=c_qt):
                    v.wait_ge(pp, c_qt)
                    for hh in range(4):
                        cp = v.tensor_copy(A(QAT + (gi * 4 + hh) * 2048 + tc * 128, 128),
                                           PT[half][:, hh * 128:(hh + 1) * 128])
                    cp.then_inc(vv, 1)
                em("vector", e_ve3e)
                set_free(f"pt{half}", "vv", inc("vv", 1))
            cnt[f"qproj{gi}"] = cur("pp")
        c_q_done = cur("vv")

        for dlc in range(4):
            w_gp = cnt.get("qproj3", 0) if dlc == 0 else cnt.get(f"latmm{dlc-2}", 0)

            sl = 2 + dlc % 2
            prior = cur(f"dg{sl}")

            def e_gp2(g, dlc=dlc, w_gp=w_gp, sl=sl, prior=prior):
                g.wait_ge(pp, w_gp)
                g.wait_ge(dg[sl], prior)
                for dc in range(16):
                    g.dma_start(out=A(WKV2 + (dlc % 2) * 2048 + dc * 128, 128),
                                in_=wkv_h[dc * 128:(dc + 1) * 128,
                                          dlc * 128:(dlc + 1) * 128]).then_inc(dg[sl], 16)
            em("gpsimd", e_gp2)
            c_wkv = inc(f"dg{sl}", 256)
            wkv_sl = sl

            w_ev = cnt.get(f"latev{dlc-1}", 0)

            def e_pe2(t, dlc=dlc, w_ev=w_ev, c_wkv=c_wkv, c_q_done=c_q_done,
                      wkv_sl=wkv_sl):
                t.wait_ge(dg[wkv_sl], c_wkv)
                if dlc == 0:
                    t.wait_ge(vv, c_q_done)
                else:
                    t.wait_ge(vv, w_ev)
                for dc in range(16):
                    for tb in range(TB):
                        mm = t.matmul(PQ[tb][:, 0:512],
                                      A(WKV2 + (dlc % 2) * 2048 + dc * 128, 128),
                                      A(LN1T + dc * 2048 + tb * 512, 512),
                                      start=(dc == 0), stop=(dc == 15))
                        if dc == 15 and tb == 3:
                            mm.then_inc(pp, 1)
            em("tensor", e_pe2)
            cnt[f"latmm{dlc}"] = inc("pp", 1)
            c_lat = cur("pp")

            def e_ve2x(v, dlc=dlc, c_lat=c_lat):
                v.wait_ge(pp, c_lat)
                for tb in range(TB):
                    cp = v.tensor_copy(A(LATT + dlc * 2048 + tb * 512, 512),
                                       PQ[tb][:, 0:512])
                cp.then_inc(vv, 1)
            em("vector", e_ve2x)
            cnt[f"latev{dlc}"] = inc("vv", 1)
        c_lat_done = cur("vv")

        g_snap2 = cur("gg")

        def e_gp4m(g, w=cnt.get("latev3", 0), g_snap2=g_snap2):
            g.wait_ge(vv, w)
            for j in range(3):
                g.memset(A(MSK + j * 512, 512), 1.0)
            g.memset(A(MSK + 3 * 512, 512), 1.0).then_inc(gg, 1)
            g.wait_ge(gg, g_snap2 + 1)
            for j in range(4):
                g.affine_select(A(MSK + j * 512, 512), A(MSK + j * 512, 512),
                                pattern=[[1, 512]], compare_op=OP.is_ge,
                                fill=0.0, base=-(j * 128), channel_multiplier=-1)
            base = A(VG, 1)
            vones = bass.AP(base.tensor, base.offset + 128,
                            [[base.ap[0][0], 128], [516, 16], [129, 4]])
            g.memset(vones, 1.0).then_inc(gg, 1)
        em("gpsimd", e_gp4m)
        c_msk = inc("gg", 2)

        for gi in range(NG):
            w_prev = cnt.get(f"kvproj{gi-1}", 0)

            sl = gi % 2
            prior = cur(f"dg{sl}")

            def e_gp4(g, gi=gi, w_prev=w_prev, w_lat=cnt.get("latev3", 0),
                      sl=sl, prior=prior):
                if gi >= 1:
                    g.wait_ge(pp, w_prev)
                else:
                    g.wait_ge(vv, w_lat)
                g.wait_ge(dg[sl], prior)
                for dlc in range(4):
                    g.dma_start(out=A(WKG + dlc * 512, 512),
                                in_=wk_h[dlc * 128:(dlc + 1) * 128,
                                         gi * 512:(gi + 1) * 512]).then_inc(dg[sl], 16)
                    g.dma_start(out=A(WVG + dlc * 512, 512),
                                in_=wv_h[dlc * 128:(dlc + 1) * 128,
                                         gi * 512:(gi + 1) * 512]).then_inc(dg[sl], 16)
            em("gpsimd", e_gp4)
            c_wkvg = inc(f"dg{sl}", 128)
            kv_sl = sl

            for tc in range(TC):
                w_k = cnt.get(f"kevac{gi}_{tc-2}", cnt.get(f"av_done{gi-1}", 0))
                w_v = cnt.get(f"vevac{gi}_{tc-2}", 0)
                w_sq = cnt.get(f"ksq{gi}_{tc-2}", 0)

                def e_pe4(t, gi=gi, tc=tc, c_wkvg=c_wkvg, w_k=w_k, w_v=w_v, w_sq=w_sq,
                          c_lat_done=c_lat_done, c_msk=c_msk, ke_snap=cur("ke"),
                          kv_sl=kv_sl):
                    if tc == 0:
                        t.wait_ge(dg[kv_sl], c_wkvg)
                        t.wait_ge(gg, c_msk)
                        if gi == 0:
                            t.wait_ge(vv, c_lat_done)
                        else:
                            t.wait_ge(ke, ke_snap)
                    if tc >= 2:
                        t.wait_ge(vv, w_k)
                        t.wait_ge(vv, w_v)
                        t.wait_ge(ss, w_sq)
                    psk = PQ[tc % 2]
                    psv = PQ[2 + tc % 2]
                    for dlc in range(4):
                        t.matmul(psk[:, 0:512], A(LATT + dlc * 2048 + tc * 128, 128),
                                 A(WKG + dlc * 512, 512), start=(dlc == 0), stop=(dlc == 3))
                        mm = t.matmul(psv[:, 0:512], A(LATT + dlc * 2048 + tc * 128, 128),
                                      A(WVG + dlc * 512, 512), start=(dlc == 0), stop=(dlc == 3))
                        if dlc == 3:
                            mm.then_inc(pp, 1)
                em("tensor", e_pe4)
                c_kv = inc("pp", 1)

                w_red = cnt.get(f"kred{gi}_{tc-1}", 0)

                def e_sc4(s, gi=gi, tc=tc, c_kv=c_kv, w_red=w_red,
                          w_evac=cur("vv") + 1):
                    if tc >= 1:
                        s.wait_ge(vv, w_red)
                    s.wait_ge(vv, w_evac)
                    s.activation(Fa(SQF, 512), A(KTOK + tc * 512, 512),
                                 AF.Square).then_inc(ss, 1)
                em("scalar", e_sc4)
                cnt[f"ksq{gi}_{tc}"] = inc("ss", 1)
                c_ks = cur("ss")

                def e_ve4a(v, gi=gi, tc=tc, c_kv=c_kv, c_ks=c_ks):
                    v.wait_ge(pp, c_kv)
                    v.tensor_copy(A(KTOK + tc * 512, 512), PQ[tc % 2][:, 0:512]).then_inc(vv, 1)
                    vg = A(VG + tc * 516, 516)
                    dst = ap3(vg, 0, [[vg.ap[0][0], 128], [129, 4], [1, 128]])
                    src = PQ[2 + tc % 2][:, 0:512]
                    src = ap3(src, 0, [[src.ap[0][0], 128], [128, 4], [1, 128]])
                    v.tensor_copy(dst, src).then_inc(vv, 1)
                    v.wait_ge(ss, c_ks)
                    for hh in range(4):
                        rd = v.tensor_reduce(SMc(SSQ + tc * 4 + hh), Fa(SQF + hh * 128, 128),
                                             axis=AX, op=OP.add)
                    rd.then_inc(vv, 1)
                em("vector", e_ve4a)
                cnt[f"kevac{gi}_{tc}"] = inc("vv", 1)
                cnt[f"vevac{gi}_{tc}"] = inc("vv", 1)
                cnt[f"kred{gi}_{tc}"] = inc("vv", 1)
            cnt[f"kvproj{gi}"] = cur("pp")
            c_allred = cur("vv")

            def e_ve4b(v, w=c_allred):
                v.wait_ge(vv, w)
                v.tensor_scalar(sm[:, SSQ:SSQ + 64], sm[:, SSQ:SSQ + 64],
                                1.0 / 128.0, 1e-6, OP.mult, OP.add).then_inc(vv, 1)
            em("vector", e_ve4b)
            c_kst = inc("vv", 1)

            def e_sc4b(s, w=c_kst):
                s.wait_ge(vv, w)
                s.activation(sm[:, SSQ:SSQ + 64], sm[:, SSQ:SSQ + 64], AF.Sqrt).then_inc(ss, 1)
            em("scalar", e_sc4b)
            c_ksd = inc("ss", 1)

            def e_ve4c(v, w=c_ksd):
                v.wait_ge(ss, w)
                v.reciprocal(sm[:, RK:RK + 64], sm[:, SSQ:SSQ + 64])
                v.drain()
                v.tensor_scalar(sm[:, RK:RK + 64], sm[:, RK:RK + 64], C_SCALE, None,
                                OP.mult).then_inc(vv, 1)
            em("vector", e_ve4c)
            c_rk = inc("vv", 1)

            for tc in range(TC):
                def e_ve4d(v, gi=gi, tc=tc, c_rk=c_rk):
                    if tc == 0:
                        v.wait_ge(vv, c_rk)
                    base = A(KTOK + tc * 512, 512)
                    pstr = base.ap[0][0]
                    x1 = ap3(base, 0, [[pstr, 128], [128, 4], [1, 32]])
                    x2 = ap3(base, 32, [[pstr, 128], [128, 4], [1, 32]])
                    fb = Fa(C4 + tc * 128, 128)
                    fstr = fb.ap[0][0]
                    ct = ap3(fb, 0, [[fstr, 128], [32, 4], [1, 32]])
                    st = ap3(Fa(S4 + tc * 128, 128), 0, [[fstr, 128], [32, 4], [1, 32]])
                    t1 = ap3(Fa(TSC, 128), 0, [[fstr, 128], [32, 4], [1, 32]])
                    t2 = ap3(Fa(TSC + 128, 128), 0, [[fstr, 128], [32, 4], [1, 32]])
                    t3 = ap3(Fa(TSC + 256, 128), 0, [[fstr, 128], [32, 4], [1, 32]])
                    t4 = ap3(Fa(TSC + 384, 128), 0, [[fstr, 128], [32, 4], [1, 32]])
                    v.drain()
                    v.tensor_tensor(t1, x1, ct, OP.mult)
                    v.tensor_tensor(t2, x2, st, OP.mult)
                    v.tensor_tensor(t3, x2, ct, OP.mult)
                    v.tensor_tensor(t4, x1, st, OP.mult)
                    v.drain()
                    v.tensor_tensor(x1, t1, t2, OP.add)
                    v.tensor_tensor(x2, t3, t4, OP.subtract).then_inc(vv, 1)
                em("vector", e_ve4d)
                c_kr = inc("vv", 1)

                half = tc % 2

                def e_pe4b(t, tc=tc, half=half, c_kr=c_kr,
                           ptw=free_at.get(f"pt{tc % 2}")):
                    t.wait_ge(vv, c_kr)
                    if ptw:
                        t.wait_ge(SEM[ptw[0]], ptw[1])
                    for hh in range(4):
                        mt = t.transpose(PT[half][:, hh * 128:(hh + 1) * 128],
                                         A(KTOK + tc * 512 + hh * 128, 128), ident[:])
                        if hh == 3:
                            mt.then_inc(pp, 1)
                em("tensor", e_pe4b)
                c_kt = inc("pp", 1)

                def e_ve4e(v, tc=tc, half=half, c_kt=c_kt):
                    v.wait_ge(pp, c_kt)
                    for hh in range(4):
                        ts_ = v.tensor_scalar(A(KGT + hh * 2048 + tc * 128, 128),
                                              PT[half][:, hh * 128:(hh + 1) * 128],
                                              SMc(KWC), None, OP.mult)
                    ts_.then_inc(vv, 1)
                em("vector", e_ve4e)
                set_free(f"pt{half}", "vv", inc("vv", 1))
            c_kT = cur("vv")

            for hh in range(4):
                hgl = gi * 4 + hh
                for qb in range(TB):
                    nkt = 4 * qb + 4
                    kq0, ke0, km0, ka0 = cur("kq"), cur("ke"), cur("km"), cur("ka")
                    kx0 = cur("kx")

                    def e_pes(t, gi=gi, hh=hh, qb=qb, nkt=nkt, kq0=kq0, ke0=ke0,
                              km0=km0, ka0=ka0, c_kT=c_kT, kx0=kx0):
                        if hh == 0 and qb == 0:
                            t.wait_ge(vv, c_kT)
                        for kt in range(nkt):
                            t.wait_ge(kx, kx0 + kt - 1 if kt >= 2 else kx0)
                            t.matmul(PQ[kt % 2][:, 0:512],
                                     A(KGT + hh * 2048 + kt * 128, 128),
                                     A(QAT + (gi * 4 + hh) * 2048 + qb * 512, 512),
                                     start=True, stop=True).then_inc(kq, 1)
                            avl = ([kt - 1] if kt >= 1 else []) + ([kt] if kt == nkt - 1 else [])
                            for avkt in avl:
                                if avkt >= 4 * qb:
                                    t.wait_ge(km, km0 + (avkt - 4 * qb) + 1)
                                else:
                                    t.wait_ge(ke, ke0 + avkt + 1)
                                for qc in range(4):
                                    pdst = (pavA if qc < 2 else pavB)[:, (qc % 2) * 129:(qc % 2) * 129 + 129]
                                    mm = t.matmul(pdst,
                                                  A(PTB + (avkt % 2) * 512 + qc * 128, 128),
                                                  A(VG + avkt * 516 + hh * 129, 129),
                                                  start=(avkt == 0 and qc % 2 == 0),
                                                  stop=(avkt == nkt - 1 and qc % 2 == 1))
                                    if qc == 3:
                                        mm.then_inc(ka, 1)
                    em("tensor", e_pes)
                    inc("kq", nkt)
                    c_avd = inc("ka", nkt)

                    def e_ved(v, gi=gi, hh=hh, qb=qb, nkt=nkt, kq0=kq0, ke0=ke0,
                              km0=km0, c_msk=c_msk,
                              first=(gi == 0 and hh == 0 and qb == 0)):
                        if first:
                            v.wait_ge(gg, c_msk)
                        for kt in range(nkt):
                            v.wait_ge(kq, kq0 + kt + 1)
                            v.wait_ge(ke, ke0 + kt - 1 if kt >= 2 else ke0)
                            v.tensor_scalar(Fa(XS + (kt % 2) * 512, 512),
                                            PQ[kt % 2][:, 0:512],
                                            SMc(RK + kt * 4 + hh), None,
                                            OP.mult).then_inc(kx, 1)
                            if kt >= 4 * qb:
                                j = kt - 4 * qb
                                v.wait_ge(ke, ke0 + kt + 1)
                                v.tensor_tensor(A(PTB + (kt % 2) * 512, 512),
                                                A(PTB + (kt % 2) * 512, 512),
                                                A(MSK + j * 512, 512),
                                                OP.mult).then_inc(km, 1)
                    em("vector", e_ved)
                    inc("kx", nkt)
                    inc("km", 4)

                    def e_scs(s, gi=gi, hh=hh, qb=qb, nkt=nkt, kx0=kx0, ka0=ka0):
                        for kt in range(nkt):
                            s.wait_ge(kx, kx0 + kt + 1)
                            s.wait_ge(ka, ka0 + kt - 1 if kt >= 2 else ka0)
                            s.activation(A(PTB + (kt % 2) * 512, 512),
                                         Fa(XS + (kt % 2) * 512, 512), AF.Exp,
                                         bias=SMc(ESH)).then_inc(ke, 1)
                    em("scalar", e_scs)
                    inc("ke", nkt)


                    def e_vee(v, c_avd=c_avd):
                        v.wait_ge(ka, c_avd)
                        for pi, pv in enumerate((pavA, pavB)):
                            base = pv[:, 0:1]
                            lsrc = bass.AP(base.tensor, base.offset + 128,
                                           [[base.ap[0][0], 128], [129, 2]])
                            v.tensor_copy(sm[:, LQ + pi * 2:LQ + pi * 2 + 2], lsrc)
                        v.drain()
                        v.reciprocal(sm[:, RL:RL + 4], sm[:, LQ:LQ + 4])
                        v.drain()
                        for qc in range(4):
                            pv = pavA if qc < 2 else pavB
                            ts_ = v.tensor_scalar(A(AVT + qc * 128, 128),
                                                  pv[:, (qc % 2) * 129:(qc % 2) * 129 + 128],
                                                  SMc(RL + qc), None, OP.mult)
                        ts_.then_inc(vv, 1)
                    em("vector", e_vee)
                    c_ave = inc("vv", 1)

                    half = qb % 2

                    def e_pet4(t, half=half, c_ave=c_ave,
                               ptw=free_at.get(f"pt{qb % 2}")):
                        t.wait_ge(vv, c_ave)
                        if ptw:
                            t.wait_ge(SEM[ptw[0]], ptw[1])
                        for qc in range(4):
                            mt = t.transpose(PT[half][:, qc * 128:(qc + 1) * 128],
                                             A(AVT + qc * 128, 128), ident[:])
                            if qc == 3:
                                mt.then_inc(pp, 1)
                    em("tensor", e_pet4)
                    c_avt = inc("pp", 1)

                    w_st = cur(f"dTs{qb % 2}")

                    def e_vet4(v, half=half, c_avt=c_avt, w_st=w_st):
                        v.wait_ge(pp, c_avt)
                        v.wait_ge(dTs[half], w_st)
                        v.tensor_copy(A(AVF + half * 512, 512),
                                      PT[half][:, 0:512]).then_inc(vv, 1)
                    em("vector", e_vet4)
                    set_free(f"pt{half}", "vv", inc("vv", 1))
                    c_avf = cur("vv")

                    prior_st = cur(f"dTs{half}")

                    def e_sys4(sy, hgl=hgl, qb=qb, c_avf=c_avf, half=half,
                               prior_st=prior_st):
                        sy.wait_ge(vv, c_avf)
                        sy.wait_ge(dTs[half], prior_st)
                        sy.dma_start(out=attn_h[hgl, :, qb * 512:(qb + 1) * 512],
                                     in_=A(AVF + half * 512, 512)).then_inc(dTs[half], 16)
                    em("sync", e_sys4)
                    cnt[f"avst{hgl}_{qb}"] = inc(f"dTs{half}", 16)
            cnt[f"av_done{gi}"] = cur("vv")
        c_attn = (cur("dTs0"), cur("dTs1"))

        def cast_slab(lo, hi):
            def e_cast(g, lo=lo, hi=hi, c_id=c_ident):
                g.wait_ge(gg, c_id)
                for i in range(lo, hi):
                    if i < 16:
                        g.dma_start(out=wob_h[i * 128:(i + 1) * 128, :],
                                    in_=wo_h[i * 128:(i + 1) * 128, :]).then_inc(dG, 16)
                    elif i < 80:
                        fc = i - 16
                        g.dma_start(out=w1b_h[fc], in_=w1p_h[fc]).then_inc(dG, 16)
                    else:
                        fc = i - 80
                        g.dma_start(out=w2b_h[fc * 128:(fc + 1) * 128, :],
                                    in_=w2t_h[fc * 128:(fc + 1) * 128, :]).then_inc(dG, 16)
            em("gpsimd", e_cast)
            return inc("dG", (hi - lo) * 16)

        c_cast = cast_slab(0, 144)

        for tb in range(TB):
            w_wo_prev = cnt.get(f"wo{tb-1}", 0)

            def e_sy5(sy, tb=tb, c_attn=c_attn, w_wo_prev=w_wo_prev):
                if tb == 0:
                    sy.wait_ge(dTs[0], c_attn[0])
                    sy.wait_ge(dTs[1], c_attn[1])
                else:
                    sy.wait_ge(pp, w_wo_prev)
                for hc in range(16):
                    sy.dma_start(out=A(ATTS + hc * 512, 512),
                                 in_=attn_h[hc, :, tb * 512:(tb + 1) * 512]).then_inc(dS, 16)
            em("sync", e_sy5)
            c_atts = inc("dS", 256)

            for pr in range(2):
                w_wop = cnt.get(f"wop{tb}_{pr-1}", cnt.get(f"wo{tb-1}", 0))

                def e_sc5w(s, tb=tb, pr=pr, c_cast=c_cast, w_wop=w_wop):
                    if tb == 0 and pr == 0:
                        s.wait_ge(dG, c_cast)
                    else:
                        s.wait_ge(pp, w_wop)
                    for hc in range(16):
                        s.dma_start(out=A(WOS + hc * 1024, 1024),
                                    in_=wob_h[hc * 128:(hc + 1) * 128,
                                              pr * 1024:(pr + 1) * 1024]).then_inc(dA, 16)
                em("scalar", e_sc5w)
                c_wol = inc("dA", 256)

                for tc in range(4):
                    w_x2p = cnt.get(f"x2{tb}_{pr}_{tc-1}",
                                    cnt.get(f"x2{tb}_{pr-1}_3",
                                            cnt.get(f"x2{tb-1}_1_3", 0)))
                    w_qz = cnt.get(f"qq{tb-1}_63", 0)

                    def e_sy5x(sy, tb=tb, pr=pr, tc=tc, w_x2p=w_x2p, w_qz=w_qz,
                               w_gl=cnt.get(f"gel{tb-1}_63", 0), w_xprior=cur("dY")):
                        if not (tb == 0 and pr == 0 and tc == 0):
                            sy.wait_ge(vv, w_x2p)
                        else:
                            sy.wait_ge(dQ, 48)
                        if tb >= 1 and pr == 0 and tc == 0:
                            sy.wait_ge(vv, w_qz)
                            sy.wait_ge(ss, w_gl)
                        sy.wait_ge(dY, w_xprior)
                        sy.dma_start(out=Fa(XD, 2048),
                                     in_=x_h[tb * 512 + tc * 128:tb * 512 + (tc + 1) * 128,
                                             :]).then_inc(dY, 16)
                    em("sync", e_sy5x)
                    c_xl = inc("dY", 16)

                    w_f2p = cnt.get(f"f2ev{tb-1}_3", 0)

                    def e_pe5(t, tb=tb, pr=pr, tc=tc, c_atts=c_atts, c_wol=c_wol,
                              w_x2p=w_x2p, w_f2p=w_f2p, ke_snap=cur("ke")):
                        if tc == 0:
                            t.wait_ge(dS, c_atts)
                            t.wait_ge(dA, c_wol)
                            if pr == 0 and tb == 0:
                                t.wait_ge(ke, ke_snap)
                            if pr == 0 and tb >= 1:
                                t.wait_ge(vv, w_f2p)
                        if not (tb == 0 and pr == 0 and tc == 0):
                            t.wait_ge(vv, w_x2p)
                        for hc in range(16):
                            t.matmul(p0[:, 0:512], A(ATTS + hc * 512 + tc * 128, 128),
                                     A(WOS + hc * 1024, 512), start=(hc == 0), stop=(hc == 15))
                            mm = t.matmul(p1[:, 0:512], A(ATTS + hc * 512 + tc * 128, 128),
                                          A(WOS + hc * 1024 + 512, 512),
                                          start=(hc == 0), stop=(hc == 15))
                            if hc == 15:
                                mm.then_inc(pp, 1)
                    em("tensor", e_pe5)
                    c_wop = inc("pp", 1)

                    def e_ve5x(v, tb=tb, pr=pr, tc=tc, c_wop=c_wop, c_xl=c_xl,
                               w_out=cnt.get(f"out{tb-1}", 0)):
                        v.wait_ge(pp, c_wop)
                        v.wait_ge(dY, c_xl)
                        if pr == 0 and tb >= 1:
                            v.wait_ge(dO, w_out)
                        v.tensor_tensor(Fa(X2B + tc * 2048 + pr * 1024, 512),
                                        p0[:, 0:512], Fa(XD + pr * 1024, 512), OP.add)
                        v.tensor_tensor(Fa(X2B + tc * 2048 + pr * 1024 + 512, 512),
                                        p1[:, 0:512], Fa(XD + pr * 1024 + 512, 512),
                                        OP.add).then_inc(vv, 1)
                    em("vector", e_ve5x)
                    cnt[f"x2{tb}_{pr}_{tc}"] = inc("vv", 1)
                cnt[f"wop{tb}_{pr}"] = cur("pp")
            cnt[f"wo{tb}"] = cur("pp")
            c_x2 = cur("vv")

            for tc in range(4):
                w_red = cnt.get(f"ln2red{tb}_{tc-1}", 0)

                def e_sc5l(s, tb=tb, tc=tc, c_x2=c_x2, w_red=w_red):
                    s.wait_ge(vv, c_x2)
                    if tc >= 1:
                        s.wait_ge(vv, w_red)
                    s.activation(Fa(XD, 2048), Fa(X2B + tc * 2048, 2048),
                                 AF.Square).then_inc(ss, 1)
                em("scalar", e_sc5l)
                c_l2s = inc("ss", 1)

                def e_ve5l(v, tb=tb, tc=tc, c_l2s=c_l2s):
                    v.wait_ge(ss, c_l2s)
                    v.reduce_sum(SMc(SXC), Fa(X2B + tc * 2048, 2048), axis=AX)
                    v.reduce_sum(SMc(SXX), Fa(XD, 2048), axis=AX).then_inc(vv, 1)
                    v.drain()
                    v.tensor_scalar(SMc(MUC), SMc(SXC), 1.0 / 2048.0, None, OP.mult)
                    v.tensor_scalar(SMc(VTMP), SMc(SXX), 1.0 / 2048.0, None, OP.mult)
                    v.drain()
                    v.scalar_tensor_tensor(SMc(SDC), SMc(MUC), SMc(MUC), SMc(VTMP),
                                           OP.mult, OP.subtract)
                    v.drain()
                    v.tensor_scalar(SMc(SDC), SMc(SDC), -1.0, 1e-5,
                                    OP.mult, OP.add).then_inc(vv, 1)
                em("vector", e_ve5l)
                cnt[f"ln2red{tb}_{tc}"] = inc("vv", 1)
                c_l2v = inc("vv", 1)

                def e_sc5l2(s, c_l2v=c_l2v):
                    s.wait_ge(vv, c_l2v)
                    s.activation(SMc(SDC), SMc(SDC), AF.Sqrt).then_inc(ss, 1)
                em("scalar", e_sc5l2)
                c_l2d = inc("ss", 1)

                def e_ve5l2(v, tb=tb, tc=tc, c_l2d=c_l2d):
                    v.wait_ge(ss, c_l2d)
                    v.reciprocal(SMc(RC), SMc(SDC))
                    v.drain()
                    v.tensor_scalar(A(LN2TOK, 2048), Fa(X2B + tc * 2048, 2048),
                                    SMc(MUC), SMc(RC), OP.subtract, OP.mult).then_inc(vv, 1)
                em("vector", e_ve5l2)
                c_l2a = inc("vv", 1)

                for b in range(4):
                    half = b % 2

                    def e_pe5t(t, b=b, half=half, c_l2a=c_l2a,
                               ptw=free_at.get(f"pt{b % 2}")):
                        t.wait_ge(vv, c_l2a)
                        if ptw:
                            t.wait_ge(SEM[ptw[0]], ptw[1])
                        for i in range(4):
                            dc = b * 4 + i
                            mt = t.transpose(PT[half][:, i * 128:(i + 1) * 128],
                                             A(LN2TOK + dc * 128, 128), ident[:])
                            if i == 3:
                                mt.then_inc(pp, 1)
                    em("tensor", e_pe5t)
                    c_l2t = inc("pp", 1)

                    def e_ve5t(v, tc=tc, b=b, half=half, c_l2t=c_l2t):
                        v.wait_ge(pp, c_l2t)
                        for i in range(4):
                            dc = b * 4 + i
                            cp = v.tensor_copy(A(LN2T + dc * 512 + tc * 128, 128),
                                               PT[half][:, i * 128:(i + 1) * 128])
                        cp.then_inc(vv, 1)
                    em("vector", e_ve5t)
                    set_free(f"pt{half}", "vv", inc("vv", 1))
            c_ln2 = cur("vv")

            for fc in range(64):
                w_f1 = cnt.get(f"ffn1{tb}_{fc-2}", 0)

                def e_sc5w1(s, tb=tb, fc=fc, w_f1=w_f1):
                    if fc >= 2:
                        s.wait_ge(pp, w_f1)
                    else:
                        s.wait_ge(dQ, 48)
                    s.dma_start(out=A(W1S + (fc % 2) * 2048, 2048),
                                in_=w1b_h[fc]).then_inc(dW[fc % 2], 16)
                em("scalar", e_sc5w1)
                c_w1l = inc(f"dW{fc % 2}", 16)
                w1sem = fc % 2

                w_hp = cnt.get(f"hp{tb}_{fc-2}", 0)

                def e_pe5f1(t, tb=tb, fc=fc, c_w1l=c_w1l, c_ln2=c_ln2, w_hp=w_hp,
                            w1sem=w1sem):
                    if fc == 0:
                        t.wait_ge(vv, c_ln2)
                    t.wait_ge(dW[w1sem], c_w1l)
                    if fc >= 2:
                        t.wait_ge(vv, w_hp)
                    ps = PQ[2 + fc % 2]
                    for dc in range(16):
                        mm = t.matmul(ps[:, 0:512], A(W1S + (fc % 2) * 2048 + dc * 128, 128),
                                      A(LN2T + dc * 512, 512), start=(dc == 0), stop=(dc == 15))
                        if dc == 15:
                            mm.then_inc(pp, 1)
                em("tensor", e_pe5f1)
                cnt[f"ffn1{tb}_{fc}"] = inc("pp", 1)
                c_f1 = cur("pp")

                w_qq = cnt.get(f"qq{tb}_{fc-2}", 0)
                w_gel = cnt.get(f"gel{tb}_{fc-2}", 0)

                def e_ve5q(v, tb=tb, fc=fc, c_f1=c_f1, w_qq=w_qq, w_gel=w_gel):
                    v.wait_ge(pp, c_f1)
                    if fc >= 2:
                        v.wait_ge(ss, w_gel)
                    xq = Fa(XQ + (fc % 2) * 512, 512)
                    v.tensor_scalar(xq, PQ[2 + fc % 2][:, 0:512], 10.0, None,
                                    OP.mult).then_inc(vv, 1)
                    v.drain()
                    v.tensor_scalar(Fa(XD, 512), xq, M64, -M64, OP.add, OP.add)
                    v.tensor_scalar(Fa(XD + 512, 512), xq, M32, -M32, OP.add, OP.add)
                    v.drain()
                    v.scalar_tensor_tensor(Fa(XD + 1024, 512), Fa(XD + 512, 512), SMc(TCB),
                                           Fa(XD, 512), OP.mult, OP.add)
                    v.drain()
                    v.tensor_scalar(Fa(XD + 1536 - (fc % 2) * 512 + (fc % 2) * 1024, 512)
                                    if False else Fa(ZS + 0, 512) if False else
                                    Fa(XD + 1536, 512) if (fc % 2) else Fa(ZS, 512),
                                    Fa(XD + 1024, 512), 32767.0, -32768.0,
                                    OP.min, OP.max).then_inc(vv, 1)
                em("vector", e_ve5q)
                cnt[f"hp{tb}_{fc}"] = inc("vv", 1)
                cnt[f"qq{tb}_{fc}"] = inc("vv", 1)
                c_qz = cur("vv")

                def e_sc5g(s, tb=tb, fc=fc, c_qz=c_qz):
                    s.wait_ge(vv, c_qz)
                    src = Fa(XD + 1536, 512) if (fc % 2) else Fa(ZS, 512)
                    s.activation(A(AT + fc * 512, 512), src,
                                 AF.Gelu_apprx_tanh, scale=0.1).then_inc(ss, 1)
                em("scalar", e_sc5g)
                cnt[f"gel{tb}_{fc}"] = inc("ss", 1)
            c_a = cur("ss")

            for dq in range(4):
                for fq in range(16):
                    w_f2 = cnt.get(f"f2{tb}_{dq}_{fq-2}", cnt.get(f"f2{tb}_{dq-1}_{fq+14}", 0))

                    def e_sc5w2(s, tb=tb, dq=dq, fq=fq, w_f2=w_f2,
                                w_pv=cnt.get(f"f2{tb-1}_3_{14}", 0)):
                        if not (dq == 0 and fq <= 1):
                            s.wait_ge(pp, w_f2)
                        elif tb >= 1:
                            s.wait_ge(pp, w_pv)
                        else:
                            s.wait_ge(dQ, 48)
                        src0 = w2b_h[0:1, 0:1]
                        src = bass.AP(src0.tensor, (fq * 512) * 2048 + dq * 512,
                                      [[2048, 128], [128 * 2048, 4], [1, 512]])
                        dstb = A(W2S + (fq % 2) * 2048, 2048)
                        dst = ap3(dstb, 0, [[dstb.ap[0][0], 128], [512, 4], [1, 512]])
                        s.dma_start(out=dst, in_=src).then_inc(dV[fq % 2], 16)
                    em("scalar", e_sc5w2)
                    c_w2l = inc(f"dV{fq % 2}", 16)
                    w2sem = fq % 2

                    w_f2ev = cnt.get(f"f2ev{tb}_{dq-1}", 0)

                    def e_pe5f2(t, tb=tb, dq=dq, fq=fq, c_w2l=c_w2l, c_a=c_a,
                                w_f2ev=w_f2ev, w2sem=w2sem):
                        if dq == 0 and fq == 0:
                            t.wait_ge(ss, c_a)
                        if fq == 0 and dq >= 1:
                            t.wait_ge(vv, w_f2ev)
                        t.wait_ge(dV[w2sem], c_w2l)
                        for fs in range(4):
                            fc = fq * 4 + fs
                            for tc in range(4):
                                mm = t.matmul(PQ[tc][:, 0:512],
                                              A(AT + fc * 512 + tc * 128, 128),
                                              A(W2S + (fq % 2) * 2048 + fs * 512, 512),
                                              start=(fc == 0), stop=(fc == 63))
                                if fs == 3 and tc == 3:
                                    mm.then_inc(pp, 1)
                    em("tensor", e_pe5f2)
                    cnt[f"f2{tb}_{dq}_{fq}"] = inc("pp", 1)
                c_f2 = cur("pp")

                def e_ve5f2(v, tb=tb, dq=dq, c_f2=c_f2):
                    v.wait_ge(pp, c_f2)
                    for tc in range(4):
                        tt = v.tensor_tensor(Fa(X2B + tc * 2048 + dq * 512, 512),
                                             Fa(X2B + tc * 2048 + dq * 512, 512),
                                             PQ[tc][:, 0:512], OP.add)
                    tt.then_inc(vv, 1)
                em("vector", e_ve5f2)
                cnt[f"f2ev{tb}_{dq}"] = inc("vv", 1)
            c_ffn = cur("vv")

            def e_out5(sy, tb=tb, c_ffn=c_ffn):
                sy.wait_ge(vv, c_ffn)
                for tc in range(4):
                    sy.dma_start(out=out_h[tb * 512 + tc * 128:tb * 512 + (tc + 1) * 128, :],
                                 in_=Fa(X2B + tc * 2048, 2048)).then_inc(dO, 16)
            em("sync", e_out5)
            cnt[f"out{tb}"] = inc("dO", 64)
        c_out = cur("dO")

        def e_fin(sy, c_out=c_out):
            sy.wait_ge(dO, c_out)
        em("sync", e_fin)

        with nc.Block() as block:
            @block.sync
            def _(sync):
                for f in plan["sync"]:
                    f(sync)

            @block.vector
            def _(vector):
                for f in plan["vector"]:
                    f(vector)

            @block.scalar
            def _(scalar):
                for f in plan["scalar"]:
                    f(scalar)

            @block.gpsimd
            def _(gpsimd):
                for f in plan["gpsimd"]:
                    f(gpsimd)

            @block.tensor
            def _(tensor):
                for f in plan["tensor"]:
                    f(tensor)

    return nc


_NC_CACHE = None


def kernel(L0_attn_x, L0_attn_wq, L0_attn_w_kv_down, L0_attn_w_k_rope,
           L0_attn_w_k_nope, L0_attn_wv, L0_attn_wo, L0_attn_q_norm_w,
           L0_attn_k_norm_w, L0_ffn_W1, L0_ffn_W2):
    global _NC_CACHE
    x = np.asarray(L0_attn_x)
    wq = np.ascontiguousarray(np.asarray(L0_attn_wq))
    wkv = np.ascontiguousarray(np.asarray(L0_attn_w_kv_down))
    wkr = np.asarray(L0_attn_w_k_rope)
    wkn = np.asarray(L0_attn_w_k_nope)
    wv = np.ascontiguousarray(np.asarray(L0_attn_wv))
    wo = np.ascontiguousarray(np.asarray(L0_attn_wo))
    W1 = np.asarray(L0_ffn_W1)
    W2 = np.asarray(L0_ffn_W2)
    qw = np.ascontiguousarray(np.asarray(L0_attn_q_norm_w).reshape(DK, 1))
    qwr = np.ascontiguousarray(np.asarray(L0_attn_q_norm_w).reshape(1, DK))
    kw = np.ascontiguousarray(np.asarray(L0_attn_k_norm_w).reshape(DK, 1))
    wk = np.empty((DL, H * DK), dtype=np.float32)
    for h in range(H):
        wk[:, h * DK:h * DK + DR] = wkr[:, h * DR:(h + 1) * DR]
        wk[:, h * DK + DR:h * DK + DK] = wkn[:, h * DR:(h + 1) * DR]
    wk = np.ascontiguousarray(wk)
    w1p = np.ascontiguousarray(
        np.asarray(W1).reshape(64, 128, 16, 128).transpose(0, 3, 2, 1).reshape(64, 128, 16 * 128))
    w2t = np.ascontiguousarray(np.asarray(W2).T)

    if _NC_CACHE is None:
        _NC_CACHE = build_nc()
    nc = _NC_CACHE
    shared = dict(wq=wq, wkv=wkv, wk=wk, wv=wv, wo=wo, w1p=w1p, w2t=w2t,
                  qw=qw, qwr=qwr, kw=kw)
    in_maps = [dict(x=np.ascontiguousarray(x[b]), **shared) for b in range(8)]
    res = run_bass_kernel_spmd(nc, in_maps, core_ids=list(range(8)))
    out = np.stack([np.asarray(res.results[b]["out"]) for b in range(8)], axis=0)
    return out.astype(np.float32)


# version 06
# speedup vs baseline: 1.0126x; 1.0126x over previous
import numpy as np
import concourse.bass as bass
import concourse.mybir as mybir
from concourse.alu_op_type import AluOpType
from concourse.bass_utils import run_bass_kernel_spmd

F32 = mybir.dt.float32
BF16 = mybir.dt.bfloat16
AF = mybir.ActivationFunctionType
AX = mybir.AxisListType.X
OP = AluOpType

S, D, H, DK, DL, DR, DFF = 2048, 2048, 16, 128, 512, 64, 8192
TC, TB, NG = 16, 4, 4
M32 = float(np.float32(1.5 * 2 ** 23))
M64 = float(1.5 * 2 ** 52)
PI_LO = float(np.nextafter(np.float32(np.pi), np.float32(0.0)))
C_SCALE = float(np.float32(1.0 / np.float32(np.sqrt(128.0))))
EXP_SHIFT = -10.0

LN1T = 0
KTOK = 0
KGT = 8192
VG = 16384
PTB = 24704
WKG = 25728
WVG = 27776
MSK = 29824
LNTOKS = 32768
QAT = 32768
QTOK = 65536
WQS = 66560
WKV2 = 65536
LATT = 69632
AVT = 77824
AVF = 78336
LN2T = 0
AT = 8192
ATTS = 40960
WOS = 49152
W1S = 65536
W2S = 69632
LN2TOK = 73728
XS = 0
SQF = 4096
C4 = 6144
S4 = 8192
TSC = 10240
X2B = 0
XD = 8192
XQ = 10240
ZS = 11264
ONES, TBL, TCB, QWC, KWC = 0, 1, 2, 3, 4
SXC, SXX, MUC, VTMP, SDC, RC = 8, 9, 10, 11, 12, 13
TPC = 14
ESH = 15
LQ, RL = 16, 24
RQ, RK, SSQ = 32, 96, 160
QWT = 256


def build_nc():
    nc = bass.Bass("TRN2", target_bir_lowering=False)
    x_h = nc.dram_tensor("x", [S, D], F32, kind="ExternalInput").ap()
    wq_h = nc.dram_tensor("wq", [D, H * DK], F32, kind="ExternalInput").ap()
    wkv_h = nc.dram_tensor("wkv", [D, DL], F32, kind="ExternalInput").ap()
    wk_h = nc.dram_tensor("wk", [DL, H * DK], F32, kind="ExternalInput").ap()
    wv_h = nc.dram_tensor("wv", [DL, H * DK], F32, kind="ExternalInput").ap()
    wo_h = nc.dram_tensor("wo", [H * DK, D], F32, kind="ExternalInput").ap()
    w1p_h = nc.dram_tensor("w1p", [64, 128, 16 * 128], F32, kind="ExternalInput").ap()
    w2t_h = nc.dram_tensor("w2t", [DFF, D], F32, kind="ExternalInput").ap()
    qw_h = nc.dram_tensor("qw", [DK, 1], F32, kind="ExternalInput").ap()
    qwr_h = nc.dram_tensor("qwr", [1, DK], F32, kind="ExternalInput").ap()
    kw_h = nc.dram_tensor("kw", [DK, 1], F32, kind="ExternalInput").ap()
    out_h = nc.dram_tensor("out", [S, D], F32, kind="ExternalOutput").ap()
    wob_h = nc.dram_tensor("wob", [H * DK, D], BF16, kind="Internal").ap()
    w1b_h = nc.dram_tensor("w1b", [64, 128, 16 * 128], BF16, kind="Internal").ap()
    w2b_h = nc.dram_tensor("w2b", [DFF, D], BF16, kind="Internal").ap()
    attn_h = nc.dram_tensor("attnh", [H, DK, S], BF16, kind="Internal").ap()
    tbl_h = nc.dram_tensor("tbl", [128, 4096], F32, kind="ExternalInput").ap()

    plan = {"sync": [], "vector": [], "scalar": [], "gpsimd": [], "tensor": []}

    def em(q, fn):
        plan[q].append(fn)

    cnt = {}

    def inc(name, n=1):
        cnt[name] = cnt.get(name, 0) + n
        return cnt[name]

    def cur(name):
        return cnt.get(name, 0)

    free_at = {}

    import contextlib
    with contextlib.ExitStack() as _stk:
        abf = _stk.enter_context(nc.sbuf_tensor("abf", [128, 79360], BF16))
        af32 = _stk.enter_context(nc.sbuf_tensor("af32", [128, 11776], F32))
        sm = _stk.enter_context(nc.sbuf_tensor("sm", [128, 384], F32))
        ident = _stk.enter_context(nc.sbuf_tensor("ident", [128, 128], BF16))
        rw = _stk.enter_context(nc.sbuf_tensor("rw", [1, 640], F32))
        p0 = _stk.enter_context(nc.psum_tensor("p0", [128, 512], F32))
        p1 = _stk.enter_context(nc.psum_tensor("p1", [128, 512], F32))
        p2 = _stk.enter_context(nc.psum_tensor("p2", [128, 512], F32))
        p3 = _stk.enter_context(nc.psum_tensor("p3", [128, 512], F32))
        pavA = _stk.enter_context(nc.psum_tensor("pavA", [128, 512], F32))
        pavB = _stk.enter_context(nc.psum_tensor("pavB", [128, 512], F32))
        pt0 = _stk.enter_context(nc.psum_tensor("pt0", [128, 1024], BF16))
        pt1 = _stk.enter_context(nc.psum_tensor("pt1", [128, 1024], BF16))
        dS = _stk.enter_context(nc.semaphore("dS"))
        dG = _stk.enter_context(nc.semaphore("dG"))
        dA = _stk.enter_context(nc.semaphore("dA"))
        dO = _stk.enter_context(nc.semaphore("dO"))
        dT = _stk.enter_context(nc.semaphore("dT"))
        dQ = _stk.enter_context(nc.semaphore("dQ"))
        dX = [_stk.enter_context(nc.semaphore(f"dX{i}")) for i in range(2)]
        dW = [_stk.enter_context(nc.semaphore(f"dW{i}")) for i in range(2)]
        dV = [_stk.enter_context(nc.semaphore(f"dV{i}")) for i in range(2)]
        dg = [_stk.enter_context(nc.semaphore(f"dg{i}")) for i in range(4)]
        dTs = [_stk.enter_context(nc.semaphore(f"dTs{i}")) for i in range(2)]
        dY = _stk.enter_context(nc.semaphore("dY"))
        kx = _stk.enter_context(nc.semaphore("kx"))
        vv = _stk.enter_context(nc.semaphore("vv"))
        ss = _stk.enter_context(nc.semaphore("ss"))
        pp = _stk.enter_context(nc.semaphore("pp"))
        gg = _stk.enter_context(nc.semaphore("gg"))
        kq = _stk.enter_context(nc.semaphore("kq"))
        ke = _stk.enter_context(nc.semaphore("ke"))
        km = _stk.enter_context(nc.semaphore("km"))
        ka = _stk.enter_context(nc.semaphore("ka"))
        PQ = [p0, p1, p2, p3]
        PT = [pt0, pt1]
        SEM = {"vv": vv, "ss": ss, "pp": pp, "gg": gg, "dS": dS, "dG": dG,
               "dA": dA, "dO": dO, "dT": dT, "dQ": dQ, "kq": kq, "ke": ke, "km": km, "ka": ka}

        def A(off, w):
            return abf[:, off:off + w]

        def Fa(off, w):
            return af32[:, off:off + w]

        def SMc(off, w=1):
            return sm[:, off:off + w]

        def wait_free(eng, res):
            if res in free_at:
                sname, c = free_at[res]
                eng.wait_ge(SEM[sname], c)

        def set_free(res, sname, c):
            free_at[res] = (sname, c)

        def ap3(apx, off_extra, dims):
            return bass.AP(apx.tensor, apx.offset + off_extra, dims)

        def e_sy0(sy):
            sy.dma_start(out=sm[:, QWC:QWC + 1], in_=qw_h).then_inc(dQ, 16)
            sy.dma_start(out=sm[:, KWC:KWC + 1], in_=kw_h).then_inc(dQ, 16)
            sy.dma_start(out=rw[:, 448:576], in_=qwr_h).then_inc(dQ, 16)
            sy.dma_start(out=af32[:, C4:C4 + 4096], in_=tbl_h).then_inc(dQ, 16)
        em("sync", e_sy0)
        c_small = inc("dQ", 64)

        g_snap = cur("gg")

        def e_gp0(g, g_snap=g_snap):
            g.memset(ident[:], 1.0)
            g.iota(rw[:, 0:32], pattern=[[1, 32]], base=0, channel_multiplier=0,
                   allow_small_or_imprecise_dtypes=True).then_inc(gg, 1)
            g.wait_ge(gg, g_snap + 1)
            g.affine_select(ident[:], ident[:], pattern=[[-1, 128]],
                            compare_op=OP.is_equal, fill=0.0,
                            base=0, channel_multiplier=1).then_inc(gg, 1)
        em("gpsimd", e_gp0)
        c_ident = inc("gg", 2)

        v_snap = cur("vv")

        def e_ve0(v, v_snap=v_snap, wg=c_ident):
            v.wait_ge(gg, wg)
            v.memset(SMc(ONES), 1.0)
            v.memset(SMc(ESH), EXP_SHIFT)
            v.memset(rw[:, 64:65], 10000.0)
            v.memset(rw[:, 65:66], 15.0)
            v.memset(rw[:, 66:67], 1.0)
            v.memset(rw[:, 144:160], 0.0)
            v.memset(rw[:, 320:448], 1.0).then_inc(vv, 1)
            v.wait_ge(vv, v_snap + 1)
            v.tensor_scalar(SMc(TBL), SMc(ONES), M64, -M64, OP.add, OP.add)
            v.reciprocal(rw[:, 67:68], rw[:, 65:66])
            v.drain()
            v.tensor_scalar(SMc(TCB), SMc(TBL), -1.0, 1.0, OP.mult, OP.add).then_inc(vv, 1)
        em("vector", e_ve0)
        c_v0 = inc("vv", 2)

        def e_sc0(s, w=c_v0):
            s.wait_ge(vv, w)
            s.activation(rw[:, 68:69], rw[:, 64:65], AF.Ln)
            s.activation(rw[:, 69:70], rw[:, 66:67], AF.Arctan).then_inc(ss, 1)
        em("scalar", e_sc0)
        c_s0 = inc("ss", 1)

        def e_ve1(v, w=c_s0, wg=c_ident):
            v.wait_ge(ss, w)
            v.wait_ge(gg, wg)
            v.tensor_scalar(rw[:, 70:71], rw[:, 69:70], 8.0, None, OP.mult)
            v.tensor_tensor(rw[:, 72:73], rw[:, 67:68], rw[:, 68:69], OP.mult)
            v.drain()
            v.reciprocal(rw[:, 71:72], rw[:, 70:71])
            v.drain()
            v.tensor_scalar(rw[:, 96:128], rw[:, 0:32], rw[:, 72:73], None,
                            OP.mult).then_inc(vv, 1)
        em("vector", e_ve1)
        c_v1 = inc("vv", 1)

        def e_sc1(s, w=c_v1):
            s.wait_ge(vv, w)
            s.activation(rw[:, 128:160], rw[:, 96:128], AF.Exp, scale=-1.0).then_inc(ss, 1)
        em("scalar", e_sc1)
        c_s1 = inc("ss", 1)

        def e_ve2(v, w=c_s1):
            v.wait_ge(ss, w)
            v.tensor_scalar(rw[:, 160:192], rw[:, 128:160], rw[:, 71:72], None,
                            OP.mult)
            v.drain()
            v.memset(rw[:, 176:192], 0.0).then_inc(vv, 1)
        em("vector", e_ve2)
        c_v2 = inc("vv", 1)

        def e_pe0(t, w1=c_v2, w2=c_small):
            t.wait_ge(vv, w1)
            t.wait_ge(dQ, w2)
            t.matmul(p3[:, 0:1], rw[0:1, 320:448], rw[0:1, 70:71], start=True, stop=True)
            t.matmul(p3[:, 128:256], rw[0:1, 320:448], rw[0:1, 448:576],
                     start=True, stop=True).then_inc(pp, 1)
        em("tensor", e_pe0)
        c_t0 = inc("pp", 1)

        def e_ve3(v, w=c_t0):
            v.wait_ge(pp, w)
            v.tensor_copy(SMc(TPC), p3[:, 0:1])
            v.tensor_copy(sm[:, QWT:QWT + 128], p3[:, 128:256]).then_inc(vv, 1)
        em("vector", e_ve3)
        c_qwt = inc("vv", 1)

        c_tables = cur("vv")

        for tc in range(TC):
            w_ap = cnt.get(f"ln1ap{tc-2}", 0)
            w_sq = cnt.get(f"ln1sq{tc-2}", 0)

            def e_sy1(sy, tc=tc, w_ap=w_ap, w_sq=w_sq, c_small=c_small):
                if tc >= 2:
                    sy.wait_ge(vv, w_ap)
                    sy.wait_ge(ss, w_sq)
                else:
                    sy.wait_ge(dQ, c_small)
                sy.dma_start(out=Fa(XS + (tc % 2) * 2048, 2048),
                             in_=x_h[tc * 128:(tc + 1) * 128, :]).then_inc(dX[tc % 2], 16)
            em("sync", e_sy1)
            xw = inc(f"dX{tc % 2}", 16)
            xsem = tc % 2

            w_red = cnt.get(f"ln1red{tc-1}", 0)

            def e_sc1a(s, tc=tc, xw=xw, w_red=w_red, xsem=xsem):
                s.wait_ge(dX[xsem], xw)
                if tc >= 1:
                    s.wait_ge(vv, w_red)
                s.activation(Fa(SQF, 2048), Fa(XS + (tc % 2) * 2048, 2048),
                             AF.Square).then_inc(ss, 1)
            em("scalar", e_sc1a)
            cnt[f"ln1sq{tc}"] = inc("ss", 1)
            c_sq = cur("ss")

            def e_ve1a(v, tc=tc, xw=xw, c_sq=c_sq, xsem=xsem):
                v.wait_ge(dX[xsem], xw)
                v.wait_ge(ss, c_sq)
                v.reduce_sum(SMc(SXC), Fa(XS + (tc % 2) * 2048, 2048), axis=AX)
                v.reduce_sum(SMc(SXX), Fa(SQF, 2048), axis=AX).then_inc(vv, 1)
                v.drain()
                v.tensor_scalar(SMc(MUC), SMc(SXC), 1.0 / 2048.0, None, OP.mult)
                v.tensor_scalar(SMc(VTMP), SMc(SXX), 1.0 / 2048.0, None, OP.mult)
                v.drain()
                v.scalar_tensor_tensor(SMc(SDC), SMc(MUC), SMc(MUC), SMc(VTMP),
                                       OP.mult, OP.subtract)
                v.drain()
                v.tensor_scalar(SMc(SDC), SMc(SDC), -1.0, 1e-5,
                                OP.mult, OP.add).then_inc(vv, 1)
            em("vector", e_ve1a)
            cnt[f"ln1red{tc}"] = inc("vv", 1)
            c_var = inc("vv", 1)

            def e_sc1b(s, c_var=c_var):
                s.wait_ge(vv, c_var)
                s.activation(SMc(SDC), SMc(SDC), AF.Sqrt).then_inc(ss, 1)
            em("scalar", e_sc1b)
            c_sd = inc("ss", 1)

            def e_ve1b(v, tc=tc, c_sd=c_sd):
                v.wait_ge(ss, c_sd)
                v.reciprocal(SMc(RC), SMc(SDC))
                v.drain()
                v.tensor_scalar(A(LNTOKS, 2048), Fa(XS + (tc % 2) * 2048, 2048),
                                SMc(MUC), SMc(RC), OP.subtract, OP.mult).then_inc(vv, 1)
            em("vector", e_ve1b)
            cnt[f"ln1ap{tc}"] = inc("vv", 1)
            c_ap = cur("vv")

            for b in range(4):
                half = b % 2

                def e_pe1(t, tc=tc, b=b, half=half, c_ap=c_ap,
                          ptw=free_at.get(f"pt{b % 2}")):
                    t.wait_ge(vv, c_ap)
                    if ptw:
                        t.wait_ge(SEM[ptw[0]], ptw[1])
                    for i in range(4):
                        dc = b * 4 + i
                        mt = t.transpose(PT[half][:, i * 128:(i + 1) * 128],
                                         A(LNTOKS + dc * 128, 128), ident[:])
                        if i == 3:
                            mt.then_inc(pp, 1)
                em("tensor", e_pe1)
                c_tp = inc("pp", 1)

                def e_ve1c(v, tc=tc, b=b, half=half, c_tp=c_tp):
                    v.wait_ge(pp, c_tp)
                    for i in range(4):
                        dc = b * 4 + i
                        cp = v.tensor_copy(A(LN1T + dc * 2048 + tc * 128, 128),
                                           PT[half][:, i * 128:(i + 1) * 128])
                    cp.then_inc(vv, 1)
                em("vector", e_ve1c)
                set_free(f"pt{half}", "vv", inc("vv", 1))
        c_ln1 = cur("vv")

        for gi in range(NG):
            w_prev = cnt.get(f"qproj{gi-1}", 0)

            sl = gi % 2
            prior = cur(f"dg{sl}")

            def e_gp3(g, gi=gi, w_prev=w_prev, c_id=c_ident, sl=sl, prior=prior):
                if gi >= 1:
                    g.wait_ge(pp, w_prev)
                else:
                    g.wait_ge(gg, c_id)
                g.wait_ge(dg[sl], prior)
                for dc in range(16):
                    g.dma_start(out=A(WQS + dc * 512, 512),
                                in_=wq_h[dc * 128:(dc + 1) * 128,
                                         gi * 512:(gi + 1) * 512]).then_inc(dg[sl], 16)
            em("gpsimd", e_gp3)
            c_wq = inc(f"dg{sl}", 256)
            wq_sl = sl

            for tc in range(TC):
                w_ev = cnt.get(f"qevac{gi}_{tc-2}", cnt.get(f"qevac{gi-1}_{14+tc}", 0))
                w_sq = cnt.get(f"qsq{gi}_{tc-2}", cnt.get(f"qsq{gi-1}_{14+tc}", 0))

                def e_pe3(t, gi=gi, tc=tc, c_wq=c_wq, w_ev=w_ev, w_sq=w_sq,
                          c_ln1=c_ln1, wq_sl=wq_sl):
                    if tc == 0:
                        t.wait_ge(dg[wq_sl], c_wq)
                        if gi == 0:
                            t.wait_ge(vv, c_ln1)
                    if gi > 0 or tc >= 2:
                        t.wait_ge(vv, w_ev)
                        t.wait_ge(ss, w_sq)
                    ps = PQ[tc % 2]
                    for dc in range(16):
                        mm = t.matmul(ps[:, 0:512],
                                      A(LN1T + dc * 2048 + tc * 128, 128),
                                      A(WQS + dc * 512, 512),
                                      start=(dc == 0), stop=(dc == 15))
                        if dc == 15:
                            mm.then_inc(pp, 1)
                em("tensor", e_pe3)
                c_qp = inc("pp", 1)

                w_red = cnt.get(f"qred{gi}_{tc-1}", cnt.get(f"qred{gi-1}_15", 0))

                def e_sc3(s, gi=gi, tc=tc, c_qp=c_qp, w_red=w_red,
                          w_evac=cur("vv") + 1):
                    if gi > 0 or tc >= 1:
                        s.wait_ge(vv, w_red)
                    s.wait_ge(vv, w_evac)
                    s.activation(Fa(SQF, 512), A(QTOK + (tc % 2) * 512, 512),
                                 AF.Square).then_inc(ss, 1)
                em("scalar", e_sc3)
                cnt[f"qsq{gi}_{tc}"] = inc("ss", 1)
                c_qs = cur("ss")

                def e_ve3a(v, gi=gi, tc=tc, c_qp=c_qp, c_qs=c_qs):
                    v.wait_ge(pp, c_qp)
                    v.tensor_copy(A(QTOK + (tc % 2) * 512, 512),
                                  PQ[tc % 2][:, 0:512]).then_inc(vv, 1)
                    v.wait_ge(ss, c_qs)
                    for hh in range(4):
                        v.tensor_reduce(SMc(SSQ + tc * 4 + hh), Fa(SQF + hh * 128, 128),
                                        axis=AX, op=OP.add)
                    v.drain()
                    v.tensor_scalar(sm[:, SSQ + tc * 4:SSQ + tc * 4 + 4],
                                    sm[:, SSQ + tc * 4:SSQ + tc * 4 + 4],
                                    1.0 / 128.0, 1e-6, OP.mult, OP.add).then_inc(vv, 1)
                em("vector", e_ve3a)
                cnt[f"qevac{gi}_{tc}"] = inc("vv", 1)
                cnt[f"qred{gi}_{tc}"] = inc("vv", 1)
                c_qst = cur("vv")

                def e_sc3rq(s, tc=tc, c_qst=c_qst):
                    s.wait_ge(vv, c_qst)
                    s.activation(sm[:, SSQ + tc * 4:SSQ + tc * 4 + 4],
                                 sm[:, SSQ + tc * 4:SSQ + tc * 4 + 4],
                                 AF.Sqrt).then_inc(ss, 1)
                em("scalar", e_sc3rq)
                c_qsd = inc("ss", 1)

                def e_ve3d(v, gi=gi, tc=tc, c_qsd=c_qsd, c_qwt=c_qwt, c_tab=c_tables,
                           c_small=c_small):
                    if gi == 0 and tc == 0:
                        v.wait_ge(vv, c_qwt)
                        v.wait_ge(vv, c_tab)
                        v.wait_ge(dQ, c_small)
                    v.wait_ge(ss, c_qsd)
                    v.reciprocal(sm[:, RQ + tc * 4:RQ + tc * 4 + 4],
                                 sm[:, SSQ + tc * 4:SSQ + tc * 4 + 4])
                    v.drain()
                    qb0 = QTOK + (tc % 2) * 512
                    for hh in range(4):
                        v.scalar_tensor_tensor(
                            A(qb0 + hh * 128, 128), A(qb0 + hh * 128, 128),
                            SMc(RQ + tc * 4 + hh),
                            sm[:, QWT:QWT + 128], OP.mult, OP.mult)
                    base = A(qb0, 512)
                    pstr = base.ap[0][0]
                    x1 = ap3(base, 0, [[pstr, 128], [128, 4], [1, 32]])
                    x2 = ap3(base, 32, [[pstr, 128], [128, 4], [1, 32]])
                    fb = Fa(C4 + tc * 128, 128)
                    fstr = fb.ap[0][0]
                    ct = ap3(fb, 0, [[fstr, 128], [32, 4], [1, 32]])
                    st = ap3(Fa(S4 + tc * 128, 128), 0, [[fstr, 128], [32, 4], [1, 32]])
                    t1 = ap3(Fa(TSC, 128), 0, [[fstr, 128], [32, 4], [1, 32]])
                    t2 = ap3(Fa(TSC + 128, 128), 0, [[fstr, 128], [32, 4], [1, 32]])
                    t3 = ap3(Fa(TSC + 256, 128), 0, [[fstr, 128], [32, 4], [1, 32]])
                    t4 = ap3(Fa(TSC + 384, 128), 0, [[fstr, 128], [32, 4], [1, 32]])
                    v.drain()
                    v.tensor_tensor(t1, x1, ct, OP.mult)
                    v.tensor_tensor(t2, x2, st, OP.mult)
                    v.tensor_tensor(t3, x2, ct, OP.mult)
                    v.tensor_tensor(t4, x1, st, OP.mult)
                    v.drain()
                    v.tensor_tensor(x1, t1, t2, OP.add)
                    v.tensor_tensor(x2, t3, t4, OP.subtract).then_inc(vv, 1)
                em("vector", e_ve3d)
                c_qr = inc("vv", 1)

                half = tc % 2

                def e_pe3b(t, tc=tc, half=half, c_qr=c_qr,
                           ptw=free_at.get(f"pt{tc % 2}")):
                    t.wait_ge(vv, c_qr)
                    if ptw:
                        t.wait_ge(SEM[ptw[0]], ptw[1])
                    qb0 = QTOK + (tc % 2) * 512
                    for hh in range(4):
                        mt = t.transpose(PT[half][:, hh * 128:(hh + 1) * 128],
                                         A(qb0 + hh * 128, 128), ident[:])
                        if hh == 3:
                            mt.then_inc(pp, 1)
                em("tensor", e_pe3b)
                c_qt = inc("pp", 1)

                def e_ve3e(v, gi=gi, tc=tc, half=half, c_qt=c_qt):
                    v.wait_ge(pp, c_qt)
                    for hh in range(4):
                        cp = v.tensor_copy(A(QAT + (gi * 4 + hh) * 2048 + tc * 128, 128),
                                           PT[half][:, hh * 128:(hh + 1) * 128])
                    cp.then_inc(vv, 1)
                em("vector", e_ve3e)
                set_free(f"pt{half}", "vv", inc("vv", 1))
            cnt[f"qproj{gi}"] = cur("pp")
        c_q_done = cur("vv")

        for dlc in range(4):
            w_gp = cnt.get("qproj3", 0) if dlc == 0 else cnt.get(f"latmm{dlc-2}", 0)

            sl = 2 + dlc % 2
            prior = cur(f"dg{sl}")

            def e_gp2(g, dlc=dlc, w_gp=w_gp, sl=sl, prior=prior):
                g.wait_ge(pp, w_gp)
                g.wait_ge(dg[sl], prior)
                for dc in range(16):
                    g.dma_start(out=A(WKV2 + (dlc % 2) * 2048 + dc * 128, 128),
                                in_=wkv_h[dc * 128:(dc + 1) * 128,
                                          dlc * 128:(dlc + 1) * 128]).then_inc(dg[sl], 16)
            em("gpsimd", e_gp2)
            c_wkv = inc(f"dg{sl}", 256)
            wkv_sl = sl

            w_ev = cnt.get(f"latev{dlc-1}", 0)

            def e_pe2(t, dlc=dlc, w_ev=w_ev, c_wkv=c_wkv, c_q_done=c_q_done,
                      wkv_sl=wkv_sl):
                t.wait_ge(dg[wkv_sl], c_wkv)
                if dlc == 0:
                    t.wait_ge(vv, c_q_done)
                else:
                    t.wait_ge(vv, w_ev)
                for dc in range(16):
                    for tb in range(TB):
                        mm = t.matmul(PQ[tb][:, 0:512],
                                      A(WKV2 + (dlc % 2) * 2048 + dc * 128, 128),
                                      A(LN1T + dc * 2048 + tb * 512, 512),
                                      start=(dc == 0), stop=(dc == 15))
                        if dc == 15 and tb == 3:
                            mm.then_inc(pp, 1)
            em("tensor", e_pe2)
            cnt[f"latmm{dlc}"] = inc("pp", 1)
            c_lat = cur("pp")

            def e_ve2x(v, dlc=dlc, c_lat=c_lat):
                v.wait_ge(pp, c_lat)
                for tb in range(TB):
                    cp = v.tensor_copy(A(LATT + dlc * 2048 + tb * 512, 512),
                                       PQ[tb][:, 0:512])
                cp.then_inc(vv, 1)
            em("vector", e_ve2x)
            cnt[f"latev{dlc}"] = inc("vv", 1)
        c_lat_done = cur("vv")

        g_snap2 = cur("gg")

        def e_gp4m(g, w=cnt.get("latev3", 0), g_snap2=g_snap2):
            g.wait_ge(vv, w)
            for j in range(3):
                g.memset(A(MSK + j * 512, 512), 1.0)
            g.memset(A(MSK + 3 * 512, 512), 1.0).then_inc(gg, 1)
            g.wait_ge(gg, g_snap2 + 1)
            for j in range(4):
                g.affine_select(A(MSK + j * 512, 512), A(MSK + j * 512, 512),
                                pattern=[[1, 512]], compare_op=OP.is_ge,
                                fill=0.0, base=-(j * 128), channel_multiplier=-1)
            base = A(VG, 1)
            vones = bass.AP(base.tensor, base.offset + 128,
                            [[base.ap[0][0], 128], [516, 16], [129, 4]])
            g.memset(vones, 1.0).then_inc(gg, 1)
        em("gpsimd", e_gp4m)
        c_msk = inc("gg", 2)

        for gi in range(NG):
            w_prev = cnt.get(f"kvproj{gi-1}", 0)

            sl = gi % 2
            prior = cur(f"dg{sl}")

            def e_gp4(g, gi=gi, w_prev=w_prev, w_lat=cnt.get("latev3", 0),
                      sl=sl, prior=prior):
                if gi >= 1:
                    g.wait_ge(pp, w_prev)
                else:
                    g.wait_ge(vv, w_lat)
                g.wait_ge(dg[sl], prior)
                for dlc in range(4):
                    g.dma_start(out=A(WKG + dlc * 512, 512),
                                in_=wk_h[dlc * 128:(dlc + 1) * 128,
                                         gi * 512:(gi + 1) * 512]).then_inc(dg[sl], 16)
                    g.dma_start(out=A(WVG + dlc * 512, 512),
                                in_=wv_h[dlc * 128:(dlc + 1) * 128,
                                         gi * 512:(gi + 1) * 512]).then_inc(dg[sl], 16)
            em("gpsimd", e_gp4)
            c_wkvg = inc(f"dg{sl}", 128)
            kv_sl = sl

            for tc in range(TC):
                w_k = cnt.get(f"kevac{gi}_{tc-2}", cnt.get(f"av_done{gi-1}", 0))
                w_v = cnt.get(f"vevac{gi}_{tc-2}", 0)
                w_sq = cnt.get(f"ksq{gi}_{tc-2}", 0)

                def e_pe4(t, gi=gi, tc=tc, c_wkvg=c_wkvg, w_k=w_k, w_v=w_v, w_sq=w_sq,
                          c_lat_done=c_lat_done, c_msk=c_msk, ke_snap=cur("ke"),
                          kv_sl=kv_sl):
                    if tc == 0:
                        t.wait_ge(dg[kv_sl], c_wkvg)
                        t.wait_ge(gg, c_msk)
                        if gi == 0:
                            t.wait_ge(vv, c_lat_done)
                        else:
                            t.wait_ge(ke, ke_snap)
                    if tc >= 2:
                        t.wait_ge(vv, w_k)
                        t.wait_ge(vv, w_v)
                        t.wait_ge(ss, w_sq)
                    psk = PQ[tc % 2]
                    psv = PQ[2 + tc % 2]
                    for dlc in range(4):
                        t.matmul(psk[:, 0:512], A(LATT + dlc * 2048 + tc * 128, 128),
                                 A(WKG + dlc * 512, 512), start=(dlc == 0), stop=(dlc == 3))
                        mm = t.matmul(psv[:, 0:512], A(LATT + dlc * 2048 + tc * 128, 128),
                                      A(WVG + dlc * 512, 512), start=(dlc == 0), stop=(dlc == 3))
                        if dlc == 3:
                            mm.then_inc(pp, 1)
                em("tensor", e_pe4)
                c_kv = inc("pp", 1)

                w_red = cnt.get(f"kred{gi}_{tc-1}", 0)

                def e_sc4(s, gi=gi, tc=tc, c_kv=c_kv, w_red=w_red,
                          w_evac=cur("vv") + 1):
                    if tc >= 1:
                        s.wait_ge(vv, w_red)
                    s.wait_ge(vv, w_evac)
                    s.activation(Fa(SQF, 512), A(KTOK + tc * 512, 512),
                                 AF.Square).then_inc(ss, 1)
                em("scalar", e_sc4)
                cnt[f"ksq{gi}_{tc}"] = inc("ss", 1)
                c_ks = cur("ss")

                def e_ve4a(v, gi=gi, tc=tc, c_kv=c_kv, c_ks=c_ks):
                    v.wait_ge(pp, c_kv)
                    v.tensor_copy(A(KTOK + tc * 512, 512), PQ[tc % 2][:, 0:512]).then_inc(vv, 1)
                    vg = A(VG + tc * 516, 516)
                    dst = ap3(vg, 0, [[vg.ap[0][0], 128], [129, 4], [1, 128]])
                    src = PQ[2 + tc % 2][:, 0:512]
                    src = ap3(src, 0, [[src.ap[0][0], 128], [128, 4], [1, 128]])
                    v.tensor_copy(dst, src).then_inc(vv, 1)
                    v.wait_ge(ss, c_ks)
                    for hh in range(4):
                        rd = v.tensor_reduce(SMc(SSQ + tc * 4 + hh), Fa(SQF + hh * 128, 128),
                                             axis=AX, op=OP.add)
                    rd.then_inc(vv, 1)
                em("vector", e_ve4a)
                cnt[f"kevac{gi}_{tc}"] = inc("vv", 1)
                cnt[f"vevac{gi}_{tc}"] = inc("vv", 1)
                cnt[f"kred{gi}_{tc}"] = inc("vv", 1)
            cnt[f"kvproj{gi}"] = cur("pp")
            c_allred = cur("vv")

            def e_ve4b(v, w=c_allred):
                v.wait_ge(vv, w)
                v.tensor_scalar(sm[:, SSQ:SSQ + 64], sm[:, SSQ:SSQ + 64],
                                1.0 / 128.0, 1e-6, OP.mult, OP.add).then_inc(vv, 1)
            em("vector", e_ve4b)
            c_kst = inc("vv", 1)

            def e_sc4b(s, w=c_kst):
                s.wait_ge(vv, w)
                s.activation(sm[:, SSQ:SSQ + 64], sm[:, SSQ:SSQ + 64], AF.Sqrt).then_inc(ss, 1)
            em("scalar", e_sc4b)
            c_ksd = inc("ss", 1)

            def e_ve4c(v, w=c_ksd):
                v.wait_ge(ss, w)
                v.reciprocal(sm[:, RK:RK + 64], sm[:, SSQ:SSQ + 64])
                v.drain()
                v.tensor_scalar(sm[:, RK:RK + 64], sm[:, RK:RK + 64], C_SCALE, None,
                                OP.mult).then_inc(vv, 1)
            em("vector", e_ve4c)
            c_rk = inc("vv", 1)

            for tc in range(TC):
                def e_ve4d(v, gi=gi, tc=tc, c_rk=c_rk):
                    if tc == 0:
                        v.wait_ge(vv, c_rk)
                    base = A(KTOK + tc * 512, 512)
                    pstr = base.ap[0][0]
                    x1 = ap3(base, 0, [[pstr, 128], [128, 4], [1, 32]])
                    x2 = ap3(base, 32, [[pstr, 128], [128, 4], [1, 32]])
                    fb = Fa(C4 + tc * 128, 128)
                    fstr = fb.ap[0][0]
                    ct = ap3(fb, 0, [[fstr, 128], [32, 4], [1, 32]])
                    st = ap3(Fa(S4 + tc * 128, 128), 0, [[fstr, 128], [32, 4], [1, 32]])
                    t1 = ap3(Fa(TSC, 128), 0, [[fstr, 128], [32, 4], [1, 32]])
                    t2 = ap3(Fa(TSC + 128, 128), 0, [[fstr, 128], [32, 4], [1, 32]])
                    t3 = ap3(Fa(TSC + 256, 128), 0, [[fstr, 128], [32, 4], [1, 32]])
                    t4 = ap3(Fa(TSC + 384, 128), 0, [[fstr, 128], [32, 4], [1, 32]])
                    v.drain()
                    v.tensor_tensor(t1, x1, ct, OP.mult)
                    v.tensor_tensor(t2, x2, st, OP.mult)
                    v.tensor_tensor(t3, x2, ct, OP.mult)
                    v.tensor_tensor(t4, x1, st, OP.mult)
                    v.drain()
                    v.tensor_tensor(x1, t1, t2, OP.add)
                    v.tensor_tensor(x2, t3, t4, OP.subtract).then_inc(vv, 1)
                em("vector", e_ve4d)
                c_kr = inc("vv", 1)

                half = tc % 2

                def e_pe4b(t, tc=tc, half=half, c_kr=c_kr,
                           ptw=free_at.get(f"pt{tc % 2}")):
                    t.wait_ge(vv, c_kr)
                    if ptw:
                        t.wait_ge(SEM[ptw[0]], ptw[1])
                    for hh in range(4):
                        mt = t.transpose(PT[half][:, hh * 128:(hh + 1) * 128],
                                         A(KTOK + tc * 512 + hh * 128, 128), ident[:])
                        if hh == 3:
                            mt.then_inc(pp, 1)
                em("tensor", e_pe4b)
                c_kt = inc("pp", 1)

                def e_ve4e(v, tc=tc, half=half, c_kt=c_kt):
                    v.wait_ge(pp, c_kt)
                    for hh in range(4):
                        ts_ = v.tensor_scalar(A(KGT + hh * 2048 + tc * 128, 128),
                                              PT[half][:, hh * 128:(hh + 1) * 128],
                                              SMc(KWC), None, OP.mult)
                    ts_.then_inc(vv, 1)
                em("vector", e_ve4e)
                set_free(f"pt{half}", "vv", inc("vv", 1))
            c_kT = cur("vv")

            for hh in range(4):
                hgl = gi * 4 + hh
                for qb in range(TB):
                    nkt = 4 * qb + 4
                    kq0, ke0, km0, ka0 = cur("kq"), cur("ke"), cur("km"), cur("ka")
                    kx0 = cur("kx")

                    def e_pes(t, gi=gi, hh=hh, qb=qb, nkt=nkt, kq0=kq0, ke0=ke0,
                              km0=km0, ka0=ka0, c_kT=c_kT, kx0=kx0):
                        if hh == 0 and qb == 0:
                            t.wait_ge(vv, c_kT)
                        for kt in range(nkt):
                            t.wait_ge(kx, kx0 + kt - 1 if kt >= 2 else kx0)
                            t.matmul(PQ[kt % 2][:, 0:512],
                                     A(KGT + hh * 2048 + kt * 128, 128),
                                     A(QAT + (gi * 4 + hh) * 2048 + qb * 512, 512),
                                     start=True, stop=True).then_inc(kq, 1)
                            avl = ([kt - 1] if kt >= 1 else []) + ([kt] if kt == nkt - 1 else [])
                            for avkt in avl:
                                if avkt >= 4 * qb:
                                    t.wait_ge(km, km0 + (avkt - 4 * qb) + 1)
                                else:
                                    t.wait_ge(ke, ke0 + avkt + 1)
                                for qc in range(4):
                                    pdst = (pavA if qc < 2 else pavB)[:, (qc % 2) * 129:(qc % 2) * 129 + 129]
                                    mm = t.matmul(pdst,
                                                  A(PTB + (avkt % 2) * 512 + qc * 128, 128),
                                                  A(VG + avkt * 516 + hh * 129, 129),
                                                  start=(avkt == 0 and qc % 2 == 0),
                                                  stop=(avkt == nkt - 1 and qc % 2 == 1))
                                    if qc == 3:
                                        mm.then_inc(ka, 1)
                    em("tensor", e_pes)
                    inc("kq", nkt)
                    c_avd = inc("ka", nkt)

                    def e_ved(v, gi=gi, hh=hh, qb=qb, nkt=nkt, kq0=kq0, ke0=ke0,
                              km0=km0, c_msk=c_msk,
                              first=(gi == 0 and hh == 0 and qb == 0)):
                        if first:
                            v.wait_ge(gg, c_msk)
                        for kt in range(nkt):
                            v.wait_ge(kq, kq0 + kt + 1)
                            v.wait_ge(ke, ke0 + kt - 1 if kt >= 2 else ke0)
                            v.tensor_scalar(Fa(XS + (kt % 2) * 512, 512),
                                            PQ[kt % 2][:, 0:512],
                                            SMc(RK + kt * 4 + hh), None,
                                            OP.mult).then_inc(kx, 1)
                            if kt >= 4 * qb:
                                j = kt - 4 * qb
                                v.wait_ge(ke, ke0 + kt + 1)
                                v.tensor_tensor(A(PTB + (kt % 2) * 512, 512),
                                                A(PTB + (kt % 2) * 512, 512),
                                                A(MSK + j * 512, 512),
                                                OP.mult).then_inc(km, 1)
                    em("vector", e_ved)
                    inc("kx", nkt)
                    inc("km", 4)

                    def e_scs(s, gi=gi, hh=hh, qb=qb, nkt=nkt, kx0=kx0, ka0=ka0):
                        for kt in range(nkt):
                            s.wait_ge(kx, kx0 + kt + 1)
                            s.wait_ge(ka, ka0 + kt - 1 if kt >= 2 else ka0)
                            s.activation(A(PTB + (kt % 2) * 512, 512),
                                         Fa(XS + (kt % 2) * 512, 512), AF.Exp,
                                         bias=SMc(ESH)).then_inc(ke, 1)
                    em("scalar", e_scs)
                    inc("ke", nkt)


                    def e_vee(v, c_avd=c_avd):
                        v.wait_ge(ka, c_avd)
                        for pi, pv in enumerate((pavA, pavB)):
                            base = pv[:, 0:1]
                            lsrc = bass.AP(base.tensor, base.offset + 128,
                                           [[base.ap[0][0], 128], [129, 2]])
                            v.tensor_copy(sm[:, LQ + pi * 2:LQ + pi * 2 + 2], lsrc)
                        v.drain()
                        v.reciprocal(sm[:, RL:RL + 4], sm[:, LQ:LQ + 4])
                        v.drain()
                        for qc in range(4):
                            pv = pavA if qc < 2 else pavB
                            ts_ = v.tensor_scalar(A(AVT + qc * 128, 128),
                                                  pv[:, (qc % 2) * 129:(qc % 2) * 129 + 128],
                                                  SMc(RL + qc), None, OP.mult)
                        ts_.then_inc(vv, 1)
                    em("vector", e_vee)
                    c_ave = inc("vv", 1)

                    half = qb % 2

                    def e_pet4(t, half=half, c_ave=c_ave,
                               ptw=free_at.get(f"pt{qb % 2}")):
                        t.wait_ge(vv, c_ave)
                        if ptw:
                            t.wait_ge(SEM[ptw[0]], ptw[1])
                        for qc in range(4):
                            mt = t.transpose(PT[half][:, qc * 128:(qc + 1) * 128],
                                             A(AVT + qc * 128, 128), ident[:])
                            if qc == 3:
                                mt.then_inc(pp, 1)
                    em("tensor", e_pet4)
                    c_avt = inc("pp", 1)

                    w_st = cur(f"dTs{qb % 2}")

                    def e_vet4(v, half=half, c_avt=c_avt, w_st=w_st):
                        v.wait_ge(pp, c_avt)
                        v.wait_ge(dTs[half], w_st)
                        v.tensor_copy(A(AVF + half * 512, 512),
                                      PT[half][:, 0:512]).then_inc(vv, 1)
                    em("vector", e_vet4)
                    set_free(f"pt{half}", "vv", inc("vv", 1))
                    c_avf = cur("vv")

                    prior_st = cur(f"dTs{half}")

                    def e_sys4(sy, hgl=hgl, qb=qb, c_avf=c_avf, half=half,
                               prior_st=prior_st):
                        sy.wait_ge(vv, c_avf)
                        sy.wait_ge(dTs[half], prior_st)
                        sy.dma_start(out=attn_h[hgl, :, qb * 512:(qb + 1) * 512],
                                     in_=A(AVF + half * 512, 512)).then_inc(dTs[half], 16)
                    em("sync", e_sys4)
                    cnt[f"avst{hgl}_{qb}"] = inc(f"dTs{half}", 16)
            cnt[f"av_done{gi}"] = cur("vv")
        c_attn = (cur("dTs0"), cur("dTs1"))

        def cast_slab(lo, hi):
            def e_cast(g, lo=lo, hi=hi, c_id=c_ident):
                g.wait_ge(gg, c_id)
                for i in range(lo, hi):
                    if i < 16:
                        g.dma_start(out=wob_h[i * 128:(i + 1) * 128, :],
                                    in_=wo_h[i * 128:(i + 1) * 128, :]).then_inc(dG, 16)
                    elif i < 80:
                        fc = i - 16
                        g.dma_start(out=w1b_h[fc], in_=w1p_h[fc]).then_inc(dG, 16)
                    else:
                        fc = i - 80
                        g.dma_start(out=w2b_h[fc * 128:(fc + 1) * 128, :],
                                    in_=w2t_h[fc * 128:(fc + 1) * 128, :]).then_inc(dG, 16)
            em("gpsimd", e_cast)
            return inc("dG", (hi - lo) * 16)

        c_cast = cast_slab(0, 144)

        for tb in range(TB):
            w_wo_prev = cnt.get(f"wo{tb-1}", 0)

            def e_sy5(sy, tb=tb, c_attn=c_attn, w_wo_prev=w_wo_prev):
                if tb == 0:
                    sy.wait_ge(dTs[0], c_attn[0])
                    sy.wait_ge(dTs[1], c_attn[1])
                else:
                    sy.wait_ge(pp, w_wo_prev)
                for hc in range(16):
                    sy.dma_start(out=A(ATTS + hc * 512, 512),
                                 in_=attn_h[hc, :, tb * 512:(tb + 1) * 512]).then_inc(dS, 16)
            em("sync", e_sy5)
            c_atts = inc("dS", 256)

            for pr in range(2):
                w_wop = cnt.get(f"wop{tb}_{pr-1}", cnt.get(f"wo{tb-1}", 0))

                def e_sc5w(s, tb=tb, pr=pr, c_cast=c_cast, w_wop=w_wop):
                    if tb == 0 and pr == 0:
                        s.wait_ge(dG, c_cast)
                    else:
                        s.wait_ge(pp, w_wop)
                    for hc in range(16):
                        s.dma_start(out=A(WOS + hc * 1024, 1024),
                                    in_=wob_h[hc * 128:(hc + 1) * 128,
                                              pr * 1024:(pr + 1) * 1024]).then_inc(dA, 16)
                em("scalar", e_sc5w)
                c_wol = inc("dA", 256)

                for tc in range(4):
                    w_x2p = cnt.get(f"x2{tb}_{pr}_{tc-1}",
                                    cnt.get(f"x2{tb}_{pr-1}_3",
                                            cnt.get(f"x2{tb-1}_1_3", 0)))
                    w_qz = cnt.get(f"qq{tb-1}_63", 0)

                    def e_sy5x(sy, tb=tb, pr=pr, tc=tc, w_x2p=w_x2p, w_qz=w_qz,
                               w_gl=cnt.get(f"gel{tb-1}_63", 0), w_xprior=cur("dY")):
                        if not (tb == 0 and pr == 0 and tc == 0):
                            sy.wait_ge(vv, w_x2p)
                        else:
                            sy.wait_ge(dQ, 64)
                        if tb >= 1 and pr == 0 and tc == 0:
                            sy.wait_ge(vv, w_qz)
                            sy.wait_ge(ss, w_gl)
                        sy.wait_ge(dY, w_xprior)
                        sy.dma_start(out=Fa(XD, 2048),
                                     in_=x_h[tb * 512 + tc * 128:tb * 512 + (tc + 1) * 128,
                                             :]).then_inc(dY, 16)
                    em("sync", e_sy5x)
                    c_xl = inc("dY", 16)

                    w_f2p = cnt.get(f"f2ev{tb-1}_3", 0)

                    def e_pe5(t, tb=tb, pr=pr, tc=tc, c_atts=c_atts, c_wol=c_wol,
                              w_x2p=w_x2p, w_f2p=w_f2p, ke_snap=cur("ke")):
                        if tc == 0:
                            t.wait_ge(dS, c_atts)
                            t.wait_ge(dA, c_wol)
                            if pr == 0 and tb == 0:
                                t.wait_ge(ke, ke_snap)
                            if pr == 0 and tb >= 1:
                                t.wait_ge(vv, w_f2p)
                        if not (tb == 0 and pr == 0 and tc == 0):
                            t.wait_ge(vv, w_x2p)
                        for hc in range(16):
                            t.matmul(p0[:, 0:512], A(ATTS + hc * 512 + tc * 128, 128),
                                     A(WOS + hc * 1024, 512), start=(hc == 0), stop=(hc == 15))
                            mm = t.matmul(p1[:, 0:512], A(ATTS + hc * 512 + tc * 128, 128),
                                          A(WOS + hc * 1024 + 512, 512),
                                          start=(hc == 0), stop=(hc == 15))
                            if hc == 15:
                                mm.then_inc(pp, 1)
                    em("tensor", e_pe5)
                    c_wop = inc("pp", 1)

                    def e_ve5x(v, tb=tb, pr=pr, tc=tc, c_wop=c_wop, c_xl=c_xl,
                               w_out=cnt.get(f"out{tb-1}", 0)):
                        v.wait_ge(pp, c_wop)
                        v.wait_ge(dY, c_xl)
                        if pr == 0 and tb >= 1:
                            v.wait_ge(dO, w_out)
                        v.tensor_tensor(Fa(X2B + tc * 2048 + pr * 1024, 512),
                                        p0[:, 0:512], Fa(XD + pr * 1024, 512), OP.add)
                        v.tensor_tensor(Fa(X2B + tc * 2048 + pr * 1024 + 512, 512),
                                        p1[:, 0:512], Fa(XD + pr * 1024 + 512, 512),
                                        OP.add).then_inc(vv, 1)
                    em("vector", e_ve5x)
                    cnt[f"x2{tb}_{pr}_{tc}"] = inc("vv", 1)
                cnt[f"wop{tb}_{pr}"] = cur("pp")
            cnt[f"wo{tb}"] = cur("pp")
            c_x2 = cur("vv")

            for tc in range(4):
                w_red = cnt.get(f"ln2red{tb}_{tc-1}", 0)

                def e_sc5l(s, tb=tb, tc=tc, c_x2=c_x2, w_red=w_red):
                    s.wait_ge(vv, c_x2)
                    if tc >= 1:
                        s.wait_ge(vv, w_red)
                    s.activation(Fa(XD, 2048), Fa(X2B + tc * 2048, 2048),
                                 AF.Square).then_inc(ss, 1)
                em("scalar", e_sc5l)
                c_l2s = inc("ss", 1)

                def e_ve5l(v, tb=tb, tc=tc, c_l2s=c_l2s):
                    v.wait_ge(ss, c_l2s)
                    v.reduce_sum(SMc(SXC), Fa(X2B + tc * 2048, 2048), axis=AX)
                    v.reduce_sum(SMc(SXX), Fa(XD, 2048), axis=AX).then_inc(vv, 1)
                    v.drain()
                    v.tensor_scalar(SMc(MUC), SMc(SXC), 1.0 / 2048.0, None, OP.mult)
                    v.tensor_scalar(SMc(VTMP), SMc(SXX), 1.0 / 2048.0, None, OP.mult)
                    v.drain()
                    v.scalar_tensor_tensor(SMc(SDC), SMc(MUC), SMc(MUC), SMc(VTMP),
                                           OP.mult, OP.subtract)
                    v.drain()
                    v.tensor_scalar(SMc(SDC), SMc(SDC), -1.0, 1e-5,
                                    OP.mult, OP.add).then_inc(vv, 1)
                em("vector", e_ve5l)
                cnt[f"ln2red{tb}_{tc}"] = inc("vv", 1)
                c_l2v = inc("vv", 1)

                def e_sc5l2(s, c_l2v=c_l2v):
                    s.wait_ge(vv, c_l2v)
                    s.activation(SMc(SDC), SMc(SDC), AF.Sqrt).then_inc(ss, 1)
                em("scalar", e_sc5l2)
                c_l2d = inc("ss", 1)

                def e_ve5l2(v, tb=tb, tc=tc, c_l2d=c_l2d):
                    v.wait_ge(ss, c_l2d)
                    v.reciprocal(SMc(RC), SMc(SDC))
                    v.drain()
                    v.tensor_scalar(A(LN2TOK, 2048), Fa(X2B + tc * 2048, 2048),
                                    SMc(MUC), SMc(RC), OP.subtract, OP.mult).then_inc(vv, 1)
                em("vector", e_ve5l2)
                c_l2a = inc("vv", 1)

                for b in range(4):
                    half = b % 2

                    def e_pe5t(t, b=b, half=half, c_l2a=c_l2a,
                               ptw=free_at.get(f"pt{b % 2}")):
                        t.wait_ge(vv, c_l2a)
                        if ptw:
                            t.wait_ge(SEM[ptw[0]], ptw[1])
                        for i in range(4):
                            dc = b * 4 + i
                            mt = t.transpose(PT[half][:, i * 128:(i + 1) * 128],
                                             A(LN2TOK + dc * 128, 128), ident[:])
                            if i == 3:
                                mt.then_inc(pp, 1)
                    em("tensor", e_pe5t)
                    c_l2t = inc("pp", 1)

                    def e_ve5t(v, tc=tc, b=b, half=half, c_l2t=c_l2t):
                        v.wait_ge(pp, c_l2t)
                        for i in range(4):
                            dc = b * 4 + i
                            cp = v.tensor_copy(A(LN2T + dc * 512 + tc * 128, 128),
                                               PT[half][:, i * 128:(i + 1) * 128])
                        cp.then_inc(vv, 1)
                    em("vector", e_ve5t)
                    set_free(f"pt{half}", "vv", inc("vv", 1))
            c_ln2 = cur("vv")

            for fc in range(64):
                w_f1 = cnt.get(f"ffn1{tb}_{fc-2}", 0)

                def e_sc5w1(s, tb=tb, fc=fc, w_f1=w_f1):
                    if fc >= 2:
                        s.wait_ge(pp, w_f1)
                    else:
                        s.wait_ge(dQ, 64)
                    s.dma_start(out=A(W1S + (fc % 2) * 2048, 2048),
                                in_=w1b_h[fc]).then_inc(dW[fc % 2], 16)
                em("scalar", e_sc5w1)
                c_w1l = inc(f"dW{fc % 2}", 16)
                w1sem = fc % 2

                w_hp = cnt.get(f"hp{tb}_{fc-2}", 0)

                def e_pe5f1(t, tb=tb, fc=fc, c_w1l=c_w1l, c_ln2=c_ln2, w_hp=w_hp,
                            w1sem=w1sem):
                    if fc == 0:
                        t.wait_ge(vv, c_ln2)
                    t.wait_ge(dW[w1sem], c_w1l)
                    if fc >= 2:
                        t.wait_ge(vv, w_hp)
                    ps = PQ[2 + fc % 2]
                    for dc in range(16):
                        mm = t.matmul(ps[:, 0:512], A(W1S + (fc % 2) * 2048 + dc * 128, 128),
                                      A(LN2T + dc * 512, 512), start=(dc == 0), stop=(dc == 15))
                        if dc == 15:
                            mm.then_inc(pp, 1)
                em("tensor", e_pe5f1)
                cnt[f"ffn1{tb}_{fc}"] = inc("pp", 1)
                c_f1 = cur("pp")

                w_qq = cnt.get(f"qq{tb}_{fc-2}", 0)
                w_gel = cnt.get(f"gel{tb}_{fc-2}", 0)

                def e_ve5q(v, tb=tb, fc=fc, c_f1=c_f1, w_qq=w_qq, w_gel=w_gel):
                    v.wait_ge(pp, c_f1)
                    if fc >= 2:
                        v.wait_ge(ss, w_gel)
                    xq = Fa(XQ + (fc % 2) * 512, 512)
                    v.tensor_scalar(xq, PQ[2 + fc % 2][:, 0:512], 10.0, None,
                                    OP.mult).then_inc(vv, 1)
                    v.drain()
                    v.tensor_scalar(Fa(XD, 512), xq, M64, -M64, OP.add, OP.add)
                    v.tensor_scalar(Fa(XD + 512, 512), xq, M32, -M32, OP.add, OP.add)
                    v.drain()
                    v.scalar_tensor_tensor(Fa(XD + 1024, 512), Fa(XD + 512, 512), SMc(TCB),
                                           Fa(XD, 512), OP.mult, OP.add)
                    v.drain()
                    v.tensor_scalar(Fa(XD + 1536 - (fc % 2) * 512 + (fc % 2) * 1024, 512)
                                    if False else Fa(ZS + 0, 512) if False else
                                    Fa(XD + 1536, 512) if (fc % 2) else Fa(ZS, 512),
                                    Fa(XD + 1024, 512), 32767.0, -32768.0,
                                    OP.min, OP.max).then_inc(vv, 1)
                em("vector", e_ve5q)
                cnt[f"hp{tb}_{fc}"] = inc("vv", 1)
                cnt[f"qq{tb}_{fc}"] = inc("vv", 1)
                c_qz = cur("vv")

                def e_sc5g(s, tb=tb, fc=fc, c_qz=c_qz):
                    s.wait_ge(vv, c_qz)
                    src = Fa(XD + 1536, 512) if (fc % 2) else Fa(ZS, 512)
                    s.activation(A(AT + fc * 512, 512), src,
                                 AF.Gelu_apprx_tanh, scale=0.1).then_inc(ss, 1)
                em("scalar", e_sc5g)
                cnt[f"gel{tb}_{fc}"] = inc("ss", 1)
            c_a = cur("ss")

            for dq in range(4):
                for fq in range(16):
                    w_f2 = cnt.get(f"f2{tb}_{dq}_{fq-2}", cnt.get(f"f2{tb}_{dq-1}_{fq+14}", 0))

                    def e_sc5w2(s, tb=tb, dq=dq, fq=fq, w_f2=w_f2,
                                w_pv=cnt.get(f"f2{tb-1}_3_{14}", 0)):
                        if not (dq == 0 and fq <= 1):
                            s.wait_ge(pp, w_f2)
                        elif tb >= 1:
                            s.wait_ge(pp, w_pv)
                        else:
                            s.wait_ge(dQ, 64)
                        src0 = w2b_h[0:1, 0:1]
                        src = bass.AP(src0.tensor, (fq * 512) * 2048 + dq * 512,
                                      [[2048, 128], [128 * 2048, 4], [1, 512]])
                        dstb = A(W2S + (fq % 2) * 2048, 2048)
                        dst = ap3(dstb, 0, [[dstb.ap[0][0], 128], [512, 4], [1, 512]])
                        s.dma_start(out=dst, in_=src).then_inc(dV[fq % 2], 16)
                    em("scalar", e_sc5w2)
                    c_w2l = inc(f"dV{fq % 2}", 16)
                    w2sem = fq % 2

                    w_f2ev = cnt.get(f"f2ev{tb}_{dq-1}", 0)

                    def e_pe5f2(t, tb=tb, dq=dq, fq=fq, c_w2l=c_w2l, c_a=c_a,
                                w_f2ev=w_f2ev, w2sem=w2sem):
                        if dq == 0 and fq == 0:
                            t.wait_ge(ss, c_a)
                        if fq == 0 and dq >= 1:
                            t.wait_ge(vv, w_f2ev)
                        t.wait_ge(dV[w2sem], c_w2l)
                        for fs in range(4):
                            fc = fq * 4 + fs
                            for tc in range(4):
                                mm = t.matmul(PQ[tc][:, 0:512],
                                              A(AT + fc * 512 + tc * 128, 128),
                                              A(W2S + (fq % 2) * 2048 + fs * 512, 512),
                                              start=(fc == 0), stop=(fc == 63))
                                if fs == 3 and tc == 3:
                                    mm.then_inc(pp, 1)
                    em("tensor", e_pe5f2)
                    cnt[f"f2{tb}_{dq}_{fq}"] = inc("pp", 1)
                c_f2 = cur("pp")

                def e_ve5f2(v, tb=tb, dq=dq, c_f2=c_f2):
                    v.wait_ge(pp, c_f2)
                    for tc in range(4):
                        tt = v.tensor_tensor(Fa(X2B + tc * 2048 + dq * 512, 512),
                                             Fa(X2B + tc * 2048 + dq * 512, 512),
                                             PQ[tc][:, 0:512], OP.add)
                    tt.then_inc(vv, 1)
                em("vector", e_ve5f2)
                cnt[f"f2ev{tb}_{dq}"] = inc("vv", 1)
            c_ffn = cur("vv")

            def e_out5(sy, tb=tb, c_ffn=c_ffn):
                sy.wait_ge(vv, c_ffn)
                for tc in range(4):
                    sy.dma_start(out=out_h[tb * 512 + tc * 128:tb * 512 + (tc + 1) * 128, :],
                                 in_=Fa(X2B + tc * 2048, 2048)).then_inc(dO, 16)
            em("sync", e_out5)
            cnt[f"out{tb}"] = inc("dO", 64)
        c_out = cur("dO")

        def e_fin(sy, c_out=c_out):
            sy.wait_ge(dO, c_out)
        em("sync", e_fin)

        with nc.Block() as block:
            @block.sync
            def _(sync):
                for f in plan["sync"]:
                    f(sync)

            @block.vector
            def _(vector):
                for f in plan["vector"]:
                    f(vector)

            @block.scalar
            def _(scalar):
                for f in plan["scalar"]:
                    f(scalar)

            @block.gpsimd
            def _(gpsimd):
                for f in plan["gpsimd"]:
                    f(gpsimd)

            @block.tensor
            def _(tensor):
                for f in plan["tensor"]:
                    f(tensor)

    return nc


_NC_CACHE = None
_TBL_CACHE = None


def _rope_tbl():
    global _TBL_CACHE
    if _TBL_CACHE is not None:
        return _TBL_CACHE
    import jax
    import jax.numpy as jnp
    with jax.default_device(jax.devices("cpu")[0]):
        quarter = DR // 4
        freq = (1.0 / 10000.0) ** jnp.linspace(0.0, 1.0, quarter, dtype=jnp.float32)
        freq = jnp.concatenate([freq, jnp.zeros((quarter,), jnp.float32)])
        theta = jnp.arange(S, dtype=jnp.float32)[:, None] * freq[None, :]
        cos = np.asarray(jnp.cos(theta))
        sin = np.asarray(jnp.sin(theta))

    def lay(t):
        a = t.reshape(16, 128, 32).transpose(1, 0, 2)[:, :, None, :]
        return np.broadcast_to(a, (128, 16, 4, 32)).reshape(128, 2048)

    _TBL_CACHE = np.ascontiguousarray(
        np.concatenate([lay(cos), lay(sin)], axis=1), dtype=np.float32)
    return _TBL_CACHE


def kernel(L0_attn_x, L0_attn_wq, L0_attn_w_kv_down, L0_attn_w_k_rope,
           L0_attn_w_k_nope, L0_attn_wv, L0_attn_wo, L0_attn_q_norm_w,
           L0_attn_k_norm_w, L0_ffn_W1, L0_ffn_W2):
    global _NC_CACHE
    x = np.asarray(L0_attn_x)
    wq = np.ascontiguousarray(np.asarray(L0_attn_wq))
    wkv = np.ascontiguousarray(np.asarray(L0_attn_w_kv_down))
    wkr = np.asarray(L0_attn_w_k_rope)
    wkn = np.asarray(L0_attn_w_k_nope)
    wv = np.ascontiguousarray(np.asarray(L0_attn_wv))
    wo = np.ascontiguousarray(np.asarray(L0_attn_wo))
    W1 = np.asarray(L0_ffn_W1)
    W2 = np.asarray(L0_ffn_W2)
    qw = np.ascontiguousarray(np.asarray(L0_attn_q_norm_w).reshape(DK, 1))
    qwr = np.ascontiguousarray(np.asarray(L0_attn_q_norm_w).reshape(1, DK))
    kw = np.ascontiguousarray(np.asarray(L0_attn_k_norm_w).reshape(DK, 1))
    wk = np.empty((DL, H * DK), dtype=np.float32)
    for h in range(H):
        wk[:, h * DK:h * DK + DR] = wkr[:, h * DR:(h + 1) * DR]
        wk[:, h * DK + DR:h * DK + DK] = wkn[:, h * DR:(h + 1) * DR]
    wk = np.ascontiguousarray(wk)
    w1p = np.ascontiguousarray(
        np.asarray(W1).reshape(64, 128, 16, 128).transpose(0, 3, 2, 1).reshape(64, 128, 16 * 128))
    w2t = np.ascontiguousarray(np.asarray(W2).T)

    if _NC_CACHE is None:
        _NC_CACHE = build_nc()
    nc = _NC_CACHE
    shared = dict(wq=wq, wkv=wkv, wk=wk, wv=wv, wo=wo, w1p=w1p, w2t=w2t,
                  qw=qw, qwr=qwr, kw=kw, tbl=_rope_tbl())
    in_maps = [dict(x=np.ascontiguousarray(x[b]), **shared) for b in range(8)]
    res = run_bass_kernel_spmd(nc, in_maps, core_ids=list(range(8)))
    out = np.stack([np.asarray(res.results[b]["out"]) for b in range(8)], axis=0)
    return out.astype(np.float32)
